# version 07
# speedup vs baseline: 1.0963x; 1.0001x over previous
import numpy as np
import contextlib
import concourse.bass as bass
import concourse.mybir as mybir
from concourse.ap import AP
from concourse.alu_op_type import AluOpType as OP
from concourse.bass_utils import run_bass_kernel_spmd

F32 = mybir.dt.float32
BF16 = mybir.dt.bfloat16
U32 = mybir.dt.uint32
AF = mybir.ActivationFunctionType
AX = mybir.AxisListType

B, S, D = 8, 2048, 1024
H, HD = 16, 64
NE, DI = 8, 4096
NCORES = 8
W = S // NCORES
CC = D // 128
IC = DI // 128
MW = 256
NMW = S // MW
RC = float(1.5 * 2 ** 23)

PHASE_C = True
DEBUG_TAPS = ()


def _tile_lhsT(wT):
    K, Mo = wT.shape
    t = wT.reshape(K // 128, 128, Mo // 128, 128).transpose(2, 1, 0, 3)
    return np.ascontiguousarray(t.reshape(Mo // 128, 128, (K // 128) * 128))


def host_prep(inputs):
    f = lambda k: np.ascontiguousarray(np.asarray(inputs[k], dtype=np.float32))
    x = f("L0_attn_x")
    sh = {}
    for nm, key in [("mixr", "L0_attn_time_mix_r"), ("mixk", "L0_attn_time_mix_k"),
                    ("mixv", "L0_attn_time_mix_v"), ("tdecay", "L0_attn_time_decay"),
                    ("tfirst", "L0_attn_time_first"), ("gn_gam", "L0_attn_ln_gamma"),
                    ("gn_bet", "L0_attn_ln_beta"), ("bo_moe", "L0_ffn_bo"),
                    ("gam_moe", "L0_ffn_gamma"), ("bet_moe", "L0_ffn_beta"),
                    ("wo1_b", "L1_attn_Wo_b"), ("b1l1", "L1_ffn_b1"),
                    ("b2l1", "L1_ffn_b2")]:
        sh[nm] = np.ascontiguousarray(f(key).reshape(-1, 1))
    sh["wrkv_t"] = np.concatenate(
        [_tile_lhsT(f("L0_attn_W_r").T), _tile_lhsT(f("L0_attn_W_k").T),
         _tile_lhsT(f("L0_attn_W_v").T)], axis=0)
    sh["wo0_t"] = _tile_lhsT(f("L0_attn_W_o").T)
    sh["wr_T"] = np.ascontiguousarray(f("L0_ffn_Wr").T)
    w1 = f("L0_ffn_W1")
    t = w1.reshape(NE, CC, 128, IC, 128).transpose(3, 2, 0, 1, 4)
    sh["w1_t"] = np.ascontiguousarray(t.reshape(IC, 128, NE * CC * 128))
    sh["b1moe"] = f("L0_ffn_b1")
    sh["womoe_t"] = _tile_lhsT(f("L0_ffn_Wo"))
    wl1 = np.concatenate([f("L1_attn_Wq_w").T, f("L1_attn_Wk_w").T,
                          f("L1_attn_Wv_w").T, f("L1_attn_Ww_w").T,
                          f("L1_attn_Wr_w").T], axis=1)
    sh["wl1_T"] = np.ascontiguousarray(wl1.reshape(CC, 128, 3104))
    sh["bl1"] = np.ascontiguousarray(np.concatenate(
        [f("L1_attn_Wq_b"), f("L1_attn_Wk_b"), f("L1_attn_Wv_b"),
         f("L1_attn_Ww_b"), f("L1_attn_Wr_b")]).reshape(1, 3104))
    sh["wo1_t"] = _tile_lhsT(f("L1_attn_Wo_w").T)
    sh["w1l1_t"] = _tile_lhsT(f("L1_ffn_w1").T)
    sh["w2l1_t"] = _tile_lhsT(f("L1_ffn_w2").T)
    in_maps = []
    for c in range(NCORES):
        m = dict(sh)
        m["x_fm"] = np.ascontiguousarray(x[c].T)
        in_maps.append(m)
    return in_maps


class Ctr:
    def __init__(self, h):
        self.h = h
        self.n = 0

    def inc(self, instr, k=1):
        instr.then_inc(self.h, k)
        self.n += k

    def wait(self, eng):
        eng.wait_ge(self.h, self.n)


class Arena:
    def __init__(self, nc, lo, hi):
        self.nc, self.free_list, self.live = nc, [(lo, hi)], {}
        self.hwm = 0

    def alloc(self, name, shape, dtype, hi_side=False):
        per_b = int(np.prod(shape[1:])) * mybir.dt.size(dtype)
        per_b = (per_b + 31) // 32 * 32
        idxs = range(len(self.free_list))
        for i in (reversed(idxs) if hi_side else idxs):
            lo, hi = self.free_list[i]
            if hi - lo >= per_b:
                if hi_side:
                    self.free_list[i] = (lo, hi - per_b)
                    base = hi - per_b
                else:
                    self.free_list[i] = (lo + per_b, hi)
                    base = lo
                self.live[name] = (base, per_b)
                self.hwm = max(self.hwm, sum(sz for _, sz in self.live.values()))
                return self.nc.alloc_sbuf_tensor_at(name, list(shape), dtype,
                                                    offset=base)
        raise MemoryError(f"SBUF arena OOM: {name} needs {per_b}B/pt; free={self.free_list}")

    def free(self, *names):
        for name in names:
            lo, sz = self.live.pop(name)
            self.free_list.append((lo, lo + sz))
        self.free_list.sort()
        out = []
        for lo, hi in self.free_list:
            if out and out[-1][1] >= lo:
                out[-1][1] = max(out[-1][1], hi)
            else:
                out.append([lo, hi])
        self.free_list = [tuple(t) for t in out]


def build_nc():
    nc = bass.Bass("TRN2", target_bir_lowering=False)
    din = lambda name, shape, dt=F32: nc.dram_tensor(name, list(shape), dt,
                                                     kind="ExternalInput")
    x_fm = din("x_fm", [D, S])
    vec_in = {nm: din(nm, [dim, 1]) for nm, dim in
              [("mixr", D), ("mixk", D), ("mixv", D), ("tdecay", D), ("tfirst", D),
               ("gn_gam", D), ("gn_bet", D), ("bo_moe", D), ("gam_moe", D),
               ("bet_moe", D), ("wo1_b", D), ("b2l1", D), ("b1l1", DI)]}
    wrkv_t = din("wrkv_t", [3 * CC, 128, 1024])
    wo0_t = din("wo0_t", [CC, 128, 1024])
    wr_T = din("wr_T", [D, NE])
    w1_t = din("w1_t", [IC, 128, NE * CC * 128])
    b1moe = din("b1moe", [NE, DI])
    womoe_t = din("womoe_t", [CC, 128, 4096])
    wl1_T = din("wl1_T", [CC, 128, 3104])
    bl1 = din("bl1", [1, 3104])
    wo1_t = din("wo1_t", [CC, 128, 1024])
    w1l1_t = din("w1l1_t", [IC, 128, 1024])
    w2l1_t = din("w2l1_t", [CC, 128, 4096])
    out_c = nc.dram_tensor("out_c", [B, W, D], F32, kind="ExternalOutput")
    taps = {tn: nc.dram_tensor("t_" + tn, [D, S], F32, kind="ExternalOutput")
            for tn in DEBUG_TAPS if tn != "cvd"}
    if "cvd" in DEBUG_TAPS:
        taps["cvd"] = nc.dram_tensor("t_cvd", [128, 20 * CC], F32,
                                     kind="ExternalOutput")

    cc_in = nc.dram_tensor("cc_in", [D, S], F32, kind="Internal")
    x1all = nc.dram_tensor("x1all", [B, D, S], F32, kind="Internal",
                           addr_space="Shared")
    w1bfd = nc.dram_tensor("w1bfd", [IC, 128, NE * CC * 128], BF16, kind="Internal")
    wobfd = nc.dram_tensor("wobfd", [CC, 128, 4096], BF16, kind="Internal")
    w1qd = nc.dram_tensor("w1qd", [IC, 128, 1024], BF16, kind="Internal")
    w2qd = nc.dram_tensor("w2qd", [CC, 128, 4096], BF16, kind="Internal")
    x1pbd = nc.dram_tensor("x1pbd", [D, S], F32, kind="Internal")
    xq1d = nc.dram_tensor("xq1d", [D, S], BF16, kind="Internal")

    ar = Arena(nc, (nc.sbuf_base + 31) // 32 * 32, nc.sbuf_top // 32 * 32)
    nc.sbuf_base = nc.sbuf_top
    A = ar.alloc

    consts = A("consts", [128, 8], F32)
    CEPS5, CEPS12, CEPS6 = 0, 1, 2
    cvec = A("cvec", [128, 20 * CC], F32)
    CV = {nm: i for i, nm in enumerate(
        ["mixr", "mixk", "mixv", "tdecay", "tfirst", "gn_gam", "gn_bet", "bo_moe",
         "gam_moe", "bet_moe", "wo1_b", "b2l1", "mr1", "mk1", "mv1", "ew", "eu",
         "gmu", "ggs", "ggb"])}

    def cv(nm, c_):
        i = CV[nm] * CC + c_
        return cvec[:, i:i + 1]

    def cvall(nm):
        i = CV[nm] * CC
        return cvec[:, i:i + CC]

    b1pp = A("b1pp", [128, IC], F32)
    rows = A("rows", [1, 6 * MW], F32)
    rowb = A("rowb", [1, 4 * MW], BF16)
    colbuf = A("colbuf", [128, 24], F32)
    ix_u32 = A("ix_u32", [128, 8], U32)
    mt_bf = A("mt_bf", [128, 8], BF16)
    ones_col = A("ones_col", [1, 128], BF16)
    onespp = A("onespp", [128, 1], F32)
    onespp_b = A("onespp_b", [128, 1], BF16)
    ones128 = A("ones128", [128, 128], BF16)
    idv = A("idv", [128, 128], BF16)
    idf = A("idf", [128, 128], F32)
    bdmask = A("bdmask", [128, 128], BF16)
    hs_bf = A("hs_bf", [128, 2], BF16)
    hst_bf = A("hst_bf", [2, 128], BF16)
    stat16 = A("stat16", [2, 48], F32)
    cb16 = A("cb16", [2, 16], BF16)
    colb16 = A("colb16", [128, 16], BF16)
    onehot8 = A("onehot8", [8, NE * 128], BF16)
    xn_bf = A("xn_bf", [128, CC, S], BF16)

    def row(i, n=MW):
        return rows[:, MW * i:MW * i + n]

    def rwb(i, n=MW):
        return rowb[:, MW * i:MW * i + n]

    RMU, RRSTD, RT0, RT1, RW0, RW1 = 0, 1, 2, 3, 4, 5

    pa = nc.alloc_psum_tensor("pa", [128, 512], F32)
    pb = nc.alloc_psum_tensor("pb", [128, 512], F32)
    pc = nc.alloc_psum_tensor("pc", [128, 512], F32)
    pd = nc.alloc_psum_tensor("pd", [128, 512], F32)
    pr1 = nc.alloc_psum_tensor("pr1", [2, 512], F32)
    pr2 = nc.alloc_psum_tensor("pr2", [2, 512], F32)
    po = nc.alloc_psum_tensor("po", [128, 128], F32)
    ptb = nc.alloc_psum_tensor("ptb", [128, 256], BF16)

    est = contextlib.ExitStack()
    sems = {nm: est.enter_context(nc.semaphore(nm)) for nm in
            ["ld", "gd", "xd", "od", "vs", "ss", "ts", "gs", "ys", "ccs",
             "wsp0", "wsp1", "wsp2", "wsp3", "wsp4", "wsp5", "wsp6", "wsp7",
             "wla", "wlb", "woa", "wob"]}
    block = est.enter_context(nc.Block())
    ld, gd, xd, od = (Ctr(sems[k]) for k in ["ld", "gd", "xd", "od"])
    vs, ss, ts, gs, ys, ccs = (Ctr(sems[k]) for k in ["vs", "ss", "ts", "gs", "ys", "ccs"])
    wspc = [Ctr(sems[f"wsp{i}"]) for i in range(8)]
    wla, wlb, woa, wob = (Ctr(sems[k]) for k in ["wla", "wlb", "woa", "wob"])
    M = {}

    def mark(label, ctr):
        M[label] = (ctr.h, ctr.n)

    def wm(eng, *labels):
        for lb in labels:
            h, n = M[lb]
            eng.wait_ge(h, n)

    _bar = [0]

    def barrier():
        tag = f"__bar{_bar[0]}"
        _bar[0] += 1

        @block.sync
        def _(sy):
            ld.wait(sy)
            ys.inc(sy.nop())

        @block.gpsimd
        def _(g):
            gd.wait(g); xd.wait(g); ccs.wait(g)
            gs.inc(g.nop())

        @block.scalar
        def _(sc):
            od.wait(sc)
            ss.inc(sc.nop())

        @block.vector
        def _(v):
            vs.inc(v.nop())

        @block.tensor
        def _(t):
            ts.inc(t.nop())

        for i, ctr in enumerate((ys, gs, ss, vs, ts)):
            mark(f"{tag}_{i}", ctr)
        labels = [f"{tag}_{i}" for i in range(5)]

        @block.sync
        def _(sy):
            ld.wait(sy)
            wm(sy, *labels)

        @block.gpsimd
        def _(g):
            wm(g, *labels)

        @block.scalar
        def _(sc):
            wm(sc, *labels)

        @block.vector
        def _(v):
            wm(v, *labels)

        @block.tensor
        def _(t):
            wm(t, *labels)

    @block.sync
    def _(sy):
        ld.wait(sy)
        with nc.allow_non_contiguous_dma("tiny one-time const vector loads"):
            for nm in ["mixr", "mixk", "mixv", "tdecay", "tfirst", "gn_gam", "gn_bet",
                       "bo_moe", "gam_moe", "bet_moe", "wo1_b", "b2l1"]:
                ld.inc(sy.dma_start(out=cvall(nm),
                                    in_=vec_in[nm].ap().rearrange("(c p) o -> p (c o)",
                                                                  p=128)), 16)
            ld.inc(sy.dma_start(out=b1pp[:, :],
                                in_=vec_in["b1l1"].ap().rearrange("(c p) o -> p (c o)",
                                                                  p=128)), 16)
        mark("cvec", ld)

    @block.gpsimd
    def _(g):
        for t_, val in [(ones128[:, :], 1.0), (ones_col[:, :], 1.0),
                        (onespp[:, :], 1.0), (onespp_b[:, :], 1.0),
                        (consts[:, CEPS5:CEPS5 + 1], 1e-5),
                        (consts[:, CEPS12:CEPS12 + 1], 1e-12),
                        (consts[:, CEPS6:CEPS6 + 1], 1e-6),
                        (bdmask[:, :], 0.0)]:
            gs.inc(g.memset(t_, val))
        gs.wait(g)
        gs.inc(g.affine_select(idv[:, :], ones128[:, :], [[-1, 128]],
                               OP.is_equal, 0.0, base=0, channel_multiplier=1))
        gs.inc(g.affine_select(hs_bf[:, :], ones128[:, 0:2], [[-64, 2]],
                               OP.is_ge, 0.0, base=0, channel_multiplier=1))
        gs.wait(g)
        gs.inc(g.affine_select(hs_bf[:, :], hs_bf[:, :], [[64, 2]],
                               OP.is_ge, 0.0, base=63, channel_multiplier=-1))
        gs.inc(g.affine_select(hst_bf[:, :], ones128[0:2, :], [[1, 128]],
                               OP.is_ge, 0.0, base=0, channel_multiplier=-64))
        gs.wait(g)
        gs.inc(g.affine_select(hst_bf[:, :], hst_bf[:, :], [[-1, 128]],
                               OP.is_ge, 0.0, base=63, channel_multiplier=64))
        gs.inc(g.affine_select(bdmask[:, :], ones128[:, :], [[-8, 16], [0, 8]],
                               OP.is_ge, 0.0, base=0, channel_multiplier=1))
        gs.wait(g)
        gs.inc(g.affine_select(bdmask[:, :], bdmask[:, :], [[8, 16], [0, 8]],
                               OP.is_ge, 0.0, base=7, channel_multiplier=-1))
        gs.inc(g.affine_select(onehot8[:, :],
                               ones128[0:8, 0:1].broadcast_to([8, NE * 128]),
                               [[-1, NE], [0, 128]],
                               OP.is_equal, 0.0, base=0, channel_multiplier=1))
        gs.wait(g)
        mark("consts_built", gs)

    @block.vector
    def _(v):
        wm(v, "cvec", "consts_built")
        vs.inc(v.tensor_copy(idf[:, :], idv[:, :]))
        for src, dst in [("mixr", "mr1"), ("mixk", "mk1"), ("mixv", "mv1")]:
            vs.inc(v.tensor_scalar(cvall(dst), cvall(src), -1.0, 1.0, OP.mult, OP.add))
        vs.wait(v)
        mark("mix1m", vs)

    @block.scalar
    def _(sc):
        wm(sc, "cvec")
        ss.inc(sc.activation(cvall("ew"), cvall("tdecay"), AF.Exp))
        ss.wait(sc)
        ss.inc(sc.mul(cvall("ew"), cvall("ew"), -1.0))
        ss.wait(sc)
        ss.inc(sc.activation(cvall("ew"), cvall("ew"), AF.Exp))
        ss.inc(sc.activation(cvall("eu"), cvall("tfirst"), AF.Exp))
        ss.wait(sc)
        mark("scanconsts", ss)

    S2 = S // 2
    maskspad = A("maskspad", [128, S], BF16)
    xtile = A("xtile", [128, CC, MW], F32, hi_side=True)
    sq_bf = A("sq_bf", [128, CC, MW], BF16, hi_side=True)
    wrt_f = A("wrt_f", [128, CC * NE], F32, hi_side=True)
    wbufA1 = A("wbufA1", [128, 8192], BF16)
    wbufB = A("wbufB", [128, 2, 1024], BF16, hi_side=True)
    ek_bf = A("ek_bf", [128, CC, S2], BF16)
    v_bf = A("v_bf", [128, CC, S2], BF16)
    sigr_bf = A("sigr_bf", [128, CC, S2], BF16)
    mixb = A("mixb", [128, CC, S2], BF16)
    ncar = A("ncar", [128, CC], F32)
    dcar = A("dcar", [128, CC], F32)
    rwkv_bf = A("rwkv_bf", [128, CC, S], BF16)

    @block.gpsimd
    def _(g):
        gs.inc(g.memset(maskspad[:, :], 0.0))
        gs.wait(g)
        mark("masks0", gs)

    @block.sync
    def _(sy):
        ld.wait(sy)
        with nc.allow_non_contiguous_dma("tiny router weight load"):
            ld.inc(sy.dma_start(
                out=wrt_f[:, :].rearrange("p (c e) -> p c e", c=CC),
                in_=wr_T.ap().rearrange("(c p) e -> p c e", p=128)), 16)
        mark("wrt", ld)

    for tw in range(NMW):
        t0 = MW * tw

        @block.sync
        def _(sy, tw=tw, t0=t0):
            ld.wait(sy)
            if tw > 0:
                wm(sy, f"xn{tw-1}", f"lastraw{tw-1}")
            for c_ in range(CC):
                ld.inc(sy.dma_start(out=xtile[:, c_, :],
                                    in_=x_fm.ap()[128 * c_:128 * c_ + 128,
                                                  t0:t0 + MW]), 16)
            mark(f"xload{tw}", ld)

        @block.scalar
        def _(sc, tw=tw):
            wm(sc, f"xload{tw}")
            if tw > 0:
                wm(sc, f"sums_mm{tw-1}")
            for c_ in range(CC):
                ss.inc(sc.activation(sq_bf[:, c_, :], xtile[:, c_, :], AF.Square))
            ss.wait(sc)
            mark(f"sq{tw}", ss)

        @block.tensor
        def _(t, tw=tw):
            wm(t, f"xload{tw}", f"sq{tw}", "consts_built")
            if tw > 0:
                wm(t, f"rows{tw-1}")
            for c_ in range(CC):
                ts.inc(t.matmul(pr1[0:1, 0:MW], onespp[:, :], xtile[:, c_, :],
                                start=(c_ == 0), stop=(c_ == CC - 1)))
            for c_ in range(CC):
                ts.inc(t.matmul(pr2[0:1, 0:MW], onespp_b[:, :], sq_bf[:, c_, :],
                                start=(c_ == 0), stop=(c_ == CC - 1)))
            ts.wait(t)
            mark(f"sums_mm{tw}", ts)

        @block.vector
        def _(v, tw=tw):
            wm(v, f"sums_mm{tw}")
            vs.inc(v.tensor_scalar(row(RMU), pr1[0:1, 0:MW], 1.0 / D, None, OP.mult))
            vs.inc(v.tensor_scalar(row(RT0), pr2[0:1, 0:MW], 1.0 / D, None, OP.mult))
            vs.wait(v)
            vs.inc(v.tensor_tensor(row(RT1), row(RMU), row(RMU), OP.mult))
            vs.wait(v)
            vs.inc(v.tensor_tensor(row(RT0), row(RT0), row(RT1), OP.subtract))
            vs.wait(v)
            mark(f"var{tw}", vs)

        @block.scalar
        def _(sc, tw=tw):
            wm(sc, f"var{tw}")
            ss.inc(sc.activation(row(RT1), row(RT0), AF.Sqrt,
                                 bias=consts[0:1, CEPS5:CEPS5 + 1]))
            ss.wait(sc)
            mark(f"sd{tw}", ss)

        @block.vector
        def _(v, tw=tw):
            wm(v, f"sd{tw}")
            vs.inc(v.reciprocal(row(RRSTD), row(RT1)))
            vs.wait(v)
            vs.inc(v.tensor_copy(rwb(0), row(RMU)))
            vs.inc(v.tensor_copy(rwb(1), row(RRSTD)))
            vs.wait(v)
            mark(f"rows{tw}", vs)

        @block.tensor
        def _(t, tw=tw):
            wm(t, f"rows{tw}")
            if tw > 0:
                wm(t, f"xn{tw-1}")
            ts.inc(t.matmul(pc[:, 0:MW], ones_col[:, :], rwb(0), start=True, stop=True))
            ts.inc(t.matmul(pd[:, 0:MW], ones_col[:, :], rwb(1), start=True, stop=True))
            ts.wait(t)
            mark(f"musd_bc{tw}", ts)

        @block.vector
        def _(v, tw=tw, t0=t0):
            wm(v, f"musd_bc{tw}")
            for c_ in range(CC):
                vs.inc(v.tensor_tensor(xtile[:, c_, :], xtile[:, c_, :],
                                       pc[:, 0:MW], OP.subtract))
                vs.wait(v)
                vs.inc(v.tensor_tensor(xtile[:, c_, :], xtile[:, c_, :],
                                       pd[:, 0:MW], OP.mult))
                vs.wait(v)
                vs.inc(v.tensor_copy(xn_bf[:, c_, t0:t0 + MW], xtile[:, c_, :]))
            vs.wait(v)
            mark(f"xn{tw}", vs)

        for sub in range(MW // 128):
            @block.tensor
            def _(t, tw=tw, sub=sub):
                wm(t, f"xn{tw}", "wrt")
                if tw + sub > 0:
                    psub = (tw, sub - 1) if sub > 0 else (tw - 1, MW // 128 - 1)
                    wm(t, f"mt{psub[0]}_{psub[1]}")
                for c_ in range(CC):
                    ts.inc(t.matmul(po[:, 0:NE],
                                    xtile[:, c_, 128 * sub:128 * sub + 128],
                                    wrt_f[:, NE * c_:NE * c_ + NE],
                                    start=(c_ == 0), stop=(c_ == CC - 1)))
                ts.wait(t)
                mark(f"raw{tw}_{sub}", ts)

            @block.vector
            def _(v, tw=tw, sub=sub):
                wm(v, f"raw{tw}_{sub}")
                vs.inc(v.tensor_copy(colbuf[:, 8:16], po[:, 0:NE]))
                vs.wait(v)
                vs.inc(v.max(colbuf[:, 0:8], colbuf[:, 8:16]))
                vs.wait(v)
                vs.inc(v.max_index(ix_u32[:, 0:8], colbuf[:, 0:8], colbuf[:, 8:16]))
                vs.wait(v)
                vs.inc(v.tensor_copy(colbuf[:, 16:17], ix_u32[:, 0:1]))
                vs.wait(v)
                for e in range(NE):
                    vs.inc(v.tensor_scalar(mt_bf[:, e:e + 1], colbuf[:, 16:17],
                                           float(e), None, OP.is_equal))
                vs.wait(v)
                mark(f"mt{tw}_{sub}", vs)

            @block.tensor
            def _(t, tw=tw, sub=sub):
                wm(t, f"mt{tw}_{sub}")
                if tw + sub > 0:
                    psub = (tw, sub - 1) if sub > 0 else (tw - 1, MW // 128 - 1)
                    wm(t, f"mcp{psub[0]}_{psub[1]}")
                ts.inc(t.transpose(ptb[0:8, 0:128], mt_bf[:, 0:8], idv[:, :]))
                ts.wait(t)
                mark(f"mtr{tw}_{sub}", ts)

            @block.vector
            def _(v, tw=tw, sub=sub, t0=t0):
                wm(v, f"mtr{tw}_{sub}", "masks0")
                vs.inc(v.tensor_copy(
                    maskspad[0:8, t0 + 128 * sub:t0 + 128 * sub + 128],
                    ptb[0:8, 0:128]))
                vs.wait(v)
                mark(f"mcp{tw}_{sub}", vs)

        @block.tensor
        def _(t, tw=tw):
            wm(t, f"mcp{tw}_{MW//128-1}")
            ts.inc(t.nop())
            ts.wait(t)
            mark(f"lastraw{tw}", ts)

    barrier()
    if "cvd" in DEBUG_TAPS:
        @block.gpsimd
        def _(g):
            xd.inc(g.dma_start(out=taps["cvd"].ap()[:, 0:17 * CC], in_=cvec[:, 0:17 * CC]), 16)
            xd.wait(g)
        barrier()
    if "xn" in DEBUG_TAPS:
        @block.gpsimd
        def _(g):
            for c_ in range(CC):
                xd.inc(g.dma_start(out=taps["xn"].ap()[128 * c_:128 * c_ + 128, :],
                                   in_=xn_bf[:, c_, :]), 16)
            xd.wait(g)
        barrier()

    ar.free("xtile", "sq_bf", "wrt_f")
    fbuf = A("fbuf", [128, S2], F32, hi_side=True)
    pnb = A("pnb", [128, S2], BF16, hi_side=True)
    nbuf = A("nbuf", [128, S2 + 4], F32, hi_side=True)
    dbuf = A("dbuf", [128, S2 + 4], F32, hi_side=True)
    numb = A("numb", [128, S2], F32, hi_side=True)
    denb = A("denb", [128, S2], F32, hi_side=True)

    PRJ = [("r", "mixr", "mr1"), ("k", "mixk", "mk1"), ("v", "mixv", "mv1")]
    for half in range(2):
        h0 = half * S2
        for prj, (pnm, mx, mx1) in enumerate(PRJ):
            hp = half * 3 + prj

            @block.gpsimd
            def _(g, hp=hp, prj=prj):
                if hp > 0:
                    wm(g, f"projmm{hp-1}")
                gd.inc(g.dma_start(
                    out=wbufA1[:, :].rearrange("p (c m) -> p c m", c=CC),
                    in_=wrkv_t.ap()[CC * prj:CC * prj + CC].rearrange(
                        "c p m -> p c m")), 16)
                gd.wait(g)
                gs.inc(g.nop())
                mark(f"wproj{hp}", gs)
                for sj in range(7 * hp, min(7 * hp + 7, IC + CC)):
                    c = wspc[sj % 8]
                    c.wait(g)
                    if sj < IC:
                        c.inc(g.dma_start(out=w1bfd.ap()[sj],
                                          in_=w1_t.ap()[sj]), 16)
                    else:
                        c.inc(g.dma_start(out=wobfd.ap()[sj - IC],
                                          in_=womoe_t.ap()[sj - IC]), 16)

            @block.vector
            def _(v, hp=hp, half=half, h0=h0, mx=mx, mx1=mx1):
                if hp > 0:
                    wm(v, f"projmm{hp-1}")
                if half == 1 and hp == 3:
                    wm(v, "scans0")
                for c_ in range(CC):
                    vs.inc(v.tensor_scalar(mixb[:, c_, :], xn_bf[:, c_, h0:h0 + S2],
                                           cv(mx, c_), None, OP.mult))
                    vs.wait(v)
                    if half == 0:
                        vs.inc(v.scalar_tensor_tensor(
                            mixb[:, c_, 1:S2], xn_bf[:, c_, 0:S2 - 1],
                            cv(mx1, c_), mixb[:, c_, 1:S2], OP.mult, OP.add))
                    else:
                        vs.inc(v.scalar_tensor_tensor(
                            mixb[:, c_, 0:S2], xn_bf[:, c_, h0 - 1:h0 + S2 - 1],
                            cv(mx1, c_), mixb[:, c_, 0:S2], OP.mult, OP.add))
                    vs.wait(v)
                mark(f"mix{hp}", vs)

            NI = CC * (S2 // 512)
            for i in range(NI):
                @block.tensor
                def _(t, hp=hp, prj=prj, i=i, NI=NI):
                    if i == 0:
                        wm(t, f"mix{hp}", f"wproj{hp}")
                    oc, twn = divmod(i, S2 // 512)
                    gi = hp * NI + i
                    pbuf = pa if i % 2 == 0 else pb
                    if gi >= 2:
                        wm(t, f"pevG{gi-2}")
                    for k in range(CC):
                        ts.inc(t.matmul(
                            pbuf[:, :],
                            wbufA1[:, 1024 * oc + 128 * k:1024 * oc + 128 * k + 128],
                            mixb[:, k, 512 * twn:512 * twn + 512],
                            start=(k == 0), stop=(k == CC - 1)))
                    ts.wait(t)
                    mark(f"pmm{hp}_{i}", ts)
                    if i == CC * (S2 // 512) - 1:
                        mark(f"projmm{hp}", ts)

                if pnm == "r":
                    @block.scalar
                    def _(sc, hp=hp, i=i):
                        oc, twn = divmod(i, S2 // 512)
                        wm(sc, f"pmm{hp}_{i}")
                        pbuf = pa if i % 2 == 0 else pb
                        ss.inc(sc.activation(
                            sigr_bf[:, oc, 512 * twn:512 * twn + 512],
                            pbuf[:, :], AF.Sigmoid))
                        ss.wait(sc)
                        mark(f"pev{hp}_{i}", ss)
                        mark(f"pevG{hp * (CC * (S2 // 512)) + i}", ss)
                elif pnm == "k":
                    @block.scalar
                    def _(sc, hp=hp, i=i):
                        oc, twn = divmod(i, S2 // 512)
                        wm(sc, f"pmm{hp}_{i}")
                        pbuf = pa if i % 2 == 0 else pb
                        ss.inc(sc.activation(
                            ek_bf[:, oc, 512 * twn:512 * twn + 512],
                            pbuf[:, :], AF.Exp))
                        ss.wait(sc)
                        mark(f"pev{hp}_{i}", ss)
                        mark(f"pevG{hp * (CC * (S2 // 512)) + i}", ss)
                else:
                    @block.vector
                    def _(v, hp=hp, i=i):
                        oc, twn = divmod(i, S2 // 512)
                        wm(v, f"pmm{hp}_{i}")
                        pbuf = pa if i % 2 == 0 else pb
                        vs.inc(v.tensor_copy(
                            v_bf[:, oc, 512 * twn:512 * twn + 512], pbuf[:, :]))
                        vs.wait(v)
                        mark(f"pev{hp}_{i}", vs)
                        mark(f"pevG{hp * (CC * (S2 // 512)) + i}", vs)

        for c_ in range(CC):
            @block.vector
            def _(v, c_=c_, half=half, h0=h0):
                wm(v, "scanconsts", f"pev{half*3+1}_{CC*(S2//512)-1}",
                   f"pev{half*3+2}_{CC*(S2//512)-1}", f"pev{half*3+0}_{CC*(S2//512)-1}")
                ew_b = cv("ew", c_).broadcast_to([128, S2])
                vs.inc(v.tensor_tensor(pnb[:, :], ek_bf[:, c_, :], v_bf[:, c_, :],
                                       OP.mult))
                if half == 0:
                    vs.inc(v.memset(nbuf[:, 0:1], 0.0))
                    vs.inc(v.memset(dbuf[:, 0:1], 0.0))
                else:
                    vs.inc(v.tensor_copy(nbuf[:, 0:1], ncar[:, c_:c_ + 1]))
                    vs.inc(v.tensor_copy(dbuf[:, 0:1], dcar[:, c_:c_ + 1]))
                vs.wait(v)
                vs.inc(v.tensor_tensor_scan(dbuf[:, 1:S2 + 1], ew_b, ek_bf[:, c_, :],
                                            dbuf[:, 0:1], OP.mult, OP.add))
                vs.wait(v)
                if half == 0:
                    vs.inc(v.tensor_copy(dcar[:, c_:c_ + 1], dbuf[:, S2:S2 + 1]))
                vs.inc(v.tensor_scalar(fbuf[:, :], dbuf[:, 0:S2], cv("ew", c_),
                                       None, OP.mult))
                vs.wait(v)
                vs.inc(v.tensor_tensor(numb[:, :], fbuf[:, :], ek_bf[:, c_, :],
                                       OP.max))
                vs.wait(v)
                vs.inc(v.reciprocal(numb[:, :], numb[:, :]))
                vs.inc(v.scalar_tensor_tensor(denb[:, :], ek_bf[:, c_, :], 1.0,
                                              fbuf[:, :], OP.mult, OP.add))
                vs.wait(v)
                vs.inc(v.tensor_tensor(fbuf[:, :], denb[:, :], numb[:, :],
                                       OP.mult))
                vs.wait(v)
                vs.inc(v.scalar_tensor_tensor(denb[:, :], ek_bf[:, c_, :],
                                              cv("eu", c_), dbuf[:, 0:S2],
                                              OP.mult, OP.add))
                vs.wait(v)
                vs.inc(v.reciprocal(denb[:, :], denb[:, :]))
                vs.inc(v.tensor_tensor(numb[:, :], fbuf[:, :], pnb[:, :],
                                       OP.mult))
                vs.wait(v)
                vs.inc(v.tensor_scalar(fbuf[:, :], fbuf[:, :], cv("ew", c_),
                                       None, OP.mult))
                vs.wait(v)
                vs.inc(v.tensor_tensor_scan(nbuf[:, 1:S2 + 1], fbuf[:, :],
                                            numb[:, :], nbuf[:, 0:1],
                                            OP.mult, OP.add))
                vs.wait(v)
                if half == 0:
                    vs.inc(v.tensor_copy(ncar[:, c_:c_ + 1], nbuf[:, S2:S2 + 1]))
                vs.inc(v.scalar_tensor_tensor(numb[:, :], pnb[:, :], cv("eu", c_),
                                              nbuf[:, 0:S2], OP.mult, OP.add))
                vs.wait(v)
                vs.inc(v.tensor_tensor(numb[:, :], numb[:, :], denb[:, :], OP.mult))
                vs.wait(v)
                vs.inc(v.tensor_tensor(rwkv_bf[:, c_, h0:h0 + S2], numb[:, :],
                                       sigr_bf[:, c_, :], OP.mult))
                vs.wait(v)
                mark(f"scan{half}_{c_}", vs)

        @block.vector
        def _(v, half=half):
            wm(v, f"scan{half}_{CC-1}")
            vs.inc(v.nop())
            vs.wait(v)
            mark(f"scans{half}", vs)

        @block.tensor
        def _(t, half=half):
            wm(t, f"scans{half}")
            ts.inc(t.nop())
            ts.wait(t)
            mark(f"projmm{half*3+2}_post", ts)

    barrier()
    for tn, buf_nm in [("rwkv", "rwkv_bf"), ("ek", "ek_bf"), ("vv", "v_bf"),
                       ("sigr", "sigr_bf")]:
        if tn in DEBUG_TAPS:
            bufs = {"rwkv_bf": rwkv_bf, "ek_bf": ek_bf, "v_bf": v_bf,
                    "sigr_bf": sigr_bf}
            buf = bufs[buf_nm]
            nw = S if tn == "rwkv" else S2

            @block.gpsimd
            def _(g, tn=tn, buf=buf, nw=nw):
                for c_ in range(CC):
                    xd.inc(g.dma_start(
                        out=taps[tn].ap()[128 * c_:128 * c_ + 128, 0:nw],
                        in_=buf[:, c_, :]), 16)
                xd.wait(g)
            barrier()
    ar.free("pnb", "nbuf", "dbuf", "numb", "denb", "mixb", "fbuf", "ek_bf", "v_bf",
            "sigr_bf", "ncar", "dcar", "wbufA1")
    nrm_bf = A("nrm_bf", [128, CC, S], BF16)
    sg_bf = A("sg_bf", [128, CC, S], BF16)
    gsq = A("gsq", [128, S], BF16, hi_side=True)

    for c_ in range(CC):
        @block.scalar
        def _(sc, c_=c_):
            if c_ > 0:
                wm(sc, f"gred{c_-1}")
            ss.inc(sc.activation(gsq[:, :], rwkv_bf[:, c_, :], AF.Square))
            ss.wait(sc)
            mark(f"gsq{c_}", ss)

        @block.vector
        def _(v, c_=c_):
            vs.inc(v.tensor_reduce(colbuf[:, c_:c_ + 1], rwkv_bf[:, c_, :],
                                   AX.X, OP.add))
            wm(v, f"gsq{c_}")
            vs.inc(v.tensor_reduce(colbuf[:, 8 + c_:9 + c_], gsq[:, :], AX.X, OP.add))
            vs.wait(v)
            mark(f"gred{c_}", vs)

    @block.vector
    def _(v):
        wm(v, f"gred{CC-1}")
        vs.inc(v.tensor_copy(colb16[:, 0:16], colbuf[:, 0:16]))
        vs.wait(v)
        mark("gcols", vs)

    @block.tensor
    def _(t):
        wm(t, "gcols")
        ts.inc(t.matmul(po[0:2, 0:8], hs_bf[:, :], colb16[:, 0:8], start=True,
                        stop=True))
        ts.inc(t.matmul(po[0:2, 8:16], hs_bf[:, :], colb16[:, 8:16],
                        start=True, stop=True))
        ts.wait(t)
        mark("ghead_mm", ts)

    @block.vector
    def _(v):
        wm(v, "ghead_mm")
        ns = float(S * HD)
        vs.inc(v.tensor_scalar(stat16[0:2, 0:16], po[0:2, 0:16], 1.0 / ns, None,
                               OP.mult))
        vs.wait(v)
        vs.inc(v.tensor_tensor(stat16[0:2, 16:24], stat16[0:2, 0:8],
                               stat16[0:2, 0:8], OP.mult))
        vs.wait(v)
        vs.inc(v.tensor_tensor(stat16[0:2, 16:24], stat16[0:2, 8:16],
                               stat16[0:2, 16:24], OP.subtract))
        vs.wait(v)
        mark("gvar", vs)

    @block.scalar
    def _(sc):
        wm(sc, "gvar")
        ss.inc(sc.activation(stat16[0:2, 24:32], stat16[0:2, 16:24], AF.Sqrt,
                             bias=consts[0:2, CEPS5:CEPS5 + 1]))
        ss.wait(sc)
        mark("gsd", ss)

    @block.vector
    def _(v):
        wm(v, "gsd")
        vs.inc(v.reciprocal(stat16[0:2, 24:32], stat16[0:2, 24:32]))
        vs.wait(v)
        vs.inc(v.tensor_copy(cb16[0:2, 0:8], stat16[0:2, 0:8]))
        vs.inc(v.tensor_copy(cb16[0:2, 8:16], stat16[0:2, 24:32]))
        vs.wait(v)
        mark("grstd", vs)

    @block.tensor
    def _(t):
        wm(t, "grstd")
        for c_ in range(CC):
            ts.inc(t.matmul(po[:, c_:c_ + 1], hst_bf[:, :], cb16[0:2, c_:c_ + 1],
                            start=True, stop=True))
            ts.inc(t.matmul(po[:, 8 + c_:9 + c_], hst_bf[:, :],
                            cb16[0:2, 8 + c_:9 + c_], start=True, stop=True))
        ts.wait(t)
        mark("gbc_mm", ts)

    @block.vector
    def _(v):
        wm(v, "gbc_mm")
        for c_ in range(CC):
            vs.inc(v.tensor_copy(cv("gmu", c_), po[:, c_:c_ + 1]))
            vs.inc(v.tensor_copy(cv("ggs", c_), po[:, 8 + c_:9 + c_]))
        vs.wait(v)
        for c_ in range(CC):
            vs.inc(v.tensor_tensor(cv("ggs", c_), cv("ggs", c_), cv("gn_gam", c_),
                                   OP.mult))
            vs.wait(v)
            vs.inc(v.tensor_tensor(cv("ggb", c_), cv("gmu", c_), cv("ggs", c_),
                                   OP.mult))
            vs.wait(v)
            vs.inc(v.tensor_tensor(cv("ggb", c_), cv("gn_bet", c_), cv("ggb", c_),
                                   OP.subtract))
            vs.wait(v)
        for c_ in range(CC):
            vs.inc(v.tensor_scalar(nrm_bf[:, c_, :], rwkv_bf[:, c_, :],
                                   cv("ggs", c_), cv("ggb", c_), OP.mult, OP.add))
        vs.wait(v)
        mark("nrm", vs)

    barrier()

    for oc in range(CC):
        @block.gpsimd
        def _(g, oc=oc):
            if oc >= 2:
                wm(g, f"womm{oc-2}")
            gd.inc(g.dma_start(out=wbufB[:, oc % 2, :], in_=wo0_t.ap()[oc]), 16)
            gd.wait(g)
            gs.inc(g.nop())
            mark(f"wo0l{oc}", gs)

        for twn in range(S // 512):
            i = oc * (S // 512) + twn

            @block.tensor
            def _(t, oc=oc, twn=twn, i=i):
                if twn == 0:
                    wm(t, f"wo0l{oc}", "nrm")
                pbuf = pa if i % 2 == 0 else pb
                if i >= 2:
                    wm(t, f"soev{i-2}")
                for k in range(CC):
                    ts.inc(t.matmul(pbuf[:, :],
                                    wbufB[:, oc % 2, 128 * k:128 * k + 128],
                                    nrm_bf[:, k, 512 * twn:512 * twn + 512],
                                    start=(k == 0), stop=(k == CC - 1)))
                ts.wait(t)
                mark(f"somm{i}", ts)
                if twn == S // 512 - 1:
                    mark(f"womm{oc}", ts)

            @block.vector
            def _(v, oc=oc, twn=twn, i=i):
                wm(v, f"somm{i}")
                pbuf = pa if i % 2 == 0 else pb
                vs.inc(v.tensor_copy(sg_bf[:, oc, 512 * twn:512 * twn + 512],
                                     pbuf[:, :]))
                vs.wait(v)
                mark(f"soev{i}", vs)

    barrier()
    if "s0" in DEBUG_TAPS:
        @block.gpsimd
        def _(g):
            for c_ in range(CC):
                xd.inc(g.dma_start(out=taps["s0"].ap()[128 * c_:128 * c_ + 128, :],
                                   in_=sg_bf[:, c_, :]), 16)
            xd.wait(g)
        barrier()

    ar.free("nrm_bf", "rwkv_bf", "gsq")
    wbufA1 = A("wbufA1", [128, 8192], BF16)
    wbufA2 = A("wbufA2", [128, 8192], BF16)
    xtile = A("xtile", [128, CC, MW], F32, hi_side=True)
    sq_bf = A("sq_bf", [128, CC, MW], BF16, hi_side=True)
    xm_bf = A("xm_bf", [128, NE, CC, MW], BF16)
    inter = A("inter", [128, IC, MW], BF16)
    hsc = A("hsc", [128, 8, MW], F32)
    yfb = A("yfb", [128, CC, MW], F32)
    gtw = A("gtw", [128, CC, MW], BF16)
    b1pad = A("b1pad", [128, DI], BF16)

    @block.gpsimd
    def _(g):
        gs.inc(g.memset(b1pad[:, :], 0.0))
        gs.wait(g)
        gd.inc(g.dma_start(out=b1pad[0:NE, :], in_=b1moe.ap()[:, :]), 16)
        gd.wait(g)
        gs.inc(g.nop())
        mark("b1pad", gs)
        for c in wspc:
            c.wait(g)
        gs.inc(g.nop())
        mark("w1spill", gs)

    for tw in range(NMW):
        t0 = MW * tw

        for e in range(NE):
            @block.tensor
            def _(t, tw=tw, t0=t0, e=e):
                if tw + e > 0:
                    pe = (tw, e - 1) if e > 0 else (tw - 1, NE - 1)
                    wm(t, f"xm{pe[0]}_{pe[1]}")
                ts.inc(t.matmul(pd[:, 0:MW], onehot8[0:8, 128 * e:128 * e + 128],
                                maskspad[0:8, t0:t0 + MW], start=True, stop=True))
                ts.wait(t)
                mark(f"mbc{tw}_{e}", ts)

            @block.vector
            def _(v, tw=tw, t0=t0, e=e):
                wm(v, f"mbc{tw}_{e}")
                if tw > 0:
                    wm(v, f"imm{tw-1}_{IC-1}")
                vs.inc(v.tensor_tensor(
                    xm_bf[:, e, :, :],
                    xn_bf[:, :, t0:t0 + MW],
                    pd[:, 0:MW].unsqueeze(1).broadcast_to([128, CC, MW]),
                    OP.mult))
                vs.wait(v)
                mark(f"xm{tw}_{e}", vs)

        for g2 in range(IC // 8):
            for k2 in range(8):
                oc = 8 * g2 + k2

                @block.sync
                def _(sy, tw=tw, oc=oc):
                    if tw == 0 and oc < 2:
                        wm(sy, "w1spill")
                    idx = tw * IC + oc
                    if idx >= 2:
                        pi = idx - 2
                        wm(sy, f"imm{pi // IC}_{pi % IC}")
                    wl = wbufA1 if oc % 2 == 0 else wbufA2
                    wc = wla if oc % 2 == 0 else wlb
                    wc.wait(sy)
                    wc.inc(sy.dma_start(out=wl[:, :], in_=w1bfd.ap()[oc]), 16)
                    mark(f"w1l{tw}_{oc}", wc)

                @block.tensor
                def _(t, tw=tw, t0=t0, oc=oc):
                    wm(t, f"w1l{tw}_{oc}")
                    if tw == 0 and oc == 0:
                        wm(t, "b1pad")
                    pbuf = pa if oc % 2 == 0 else pb
                    idx = tw * IC + oc
                    if idx >= 2:
                        pi = idx - 2
                        wm(t, f"pev{pi // IC}_{pi % IC}")
                    for e in range(NE):
                        wm(t, f"xm{tw}_{e}")
                        for k in range(CC):
                            wl = wbufA1 if oc % 2 == 0 else wbufA2
                            ts.inc(t.matmul(pbuf[:, 0:MW],
                                            wl[:, 1024 * e + 128 * k:
                                               1024 * e + 128 * k + 128],
                                            xm_bf[:, e, k, :],
                                            start=(e == 0 and k == 0), stop=False))
                    ts.inc(t.matmul(pbuf[:, 0:MW],
                                    b1pad[:, 128 * oc:128 * oc + 128],
                                    maskspad[:, t0:t0 + MW], start=False,
                                    stop=True))
                    mark(f"imm{tw}_{oc}", ts)

                @block.vector
                def _(v, tw=tw, oc=oc):
                    wm(v, f"imm{tw}_{oc}")
                    gi = oc // 8
                    if gi > 0:
                        wm(v, f"ievg{tw}_{gi-1}")
                    elif tw > 0:
                        wm(v, f"ievg{tw-1}_{IC//8-1}")
                    pbuf = pa if oc % 2 == 0 else pb
                    vs.inc(v.tensor_copy(hsc[:, oc % 8, :], pbuf[:, 0:MW]))
                    vs.wait(v)
                    mark(f"pev{tw}_{oc}", vs)

            @block.scalar
            def _(sc, tw=tw, g2=g2):
                if g2 == 0 and tw > 0:
                    wm(sc, f"ypart{tw-1}_{CC-1}_3")
                for k2 in range(8):
                    oc = 8 * g2 + k2
                    wm(sc, f"pev{tw}_{oc}")
                    ss.inc(sc.activation(inter[:, oc, :], hsc[:, oc % 8, :],
                                         AF.Gelu))
                    mark(f"iev{tw}_{oc}", ss)
                ss.wait(sc)
                mark(f"ievg{tw}_{g2}", ss)

        for oc2 in range(CC):
            for part in range(4):
                @block.sync
                def _(sy, tw=tw, oc2=oc2, part=part):
                    if tw == 0 and oc2 == 0 and part == 0:
                        wm(sy, "w1spill")
                    idx = (tw * CC + oc2) * 4 + part
                    if idx >= 2:
                        pi = idx - 2
                        po2, pp2 = divmod(pi % (CC * 4), 4)
                        wm(sy, f"ypart{pi // (CC*4)}_{po2}_{pp2}")
                    wc = woa if part % 2 == 0 else wob
                    wc.wait(sy)
                    wc.inc(sy.dma_start(out=wbufB[:, part % 2, :],
                                        in_=wobfd.ap()[oc2][:, 1024 * part:
                                                            1024 * part + 1024]), 16)
                    mark(f"wop{tw}_{oc2}_{part}", wc)

                @block.tensor
                def _(t, tw=tw, oc2=oc2, part=part):
                    wm(t, f"wop{tw}_{oc2}_{part}")
                    if part == 0:
                        wm(t, f"iev{tw}_{IC-1}")
                        if tw * CC + oc2 >= 1:
                            pi = tw * CC + oc2 - 1
                            wm(t, f"yev{pi // CC}_{pi % CC}")
                    for kk in range(8):
                        k = part * 8 + kk
                        ts.inc(t.matmul(pc[:, 0:MW],
                                        wbufB[:, part % 2, 128 * kk:128 * kk + 128],
                                        inter[:, k, :],
                                        start=(k == 0), stop=(k == IC - 1)))
                    mark(f"ypart{tw}_{oc2}_{part}", ts)

            @block.vector
            def _(v, tw=tw, t0=t0, oc2=oc2):
                wm(v, f"ypart{tw}_{oc2}_3")
                pbuf = pc
                vs.inc(v.scalar_tensor_tensor(yfb[:, oc2, :], pbuf[:, 0:MW],
                                              cv("bo_moe", oc2),
                                              xn_bf[:, oc2, t0:t0 + MW],
                                              OP.add, OP.add))
                vs.wait(v)
                mark(f"yev{tw}_{oc2}", vs)
                if oc2 == CC - 1:
                    mark(f"yf{tw}", vs)

        @block.scalar
        def _(sc, tw=tw):
            wm(sc, f"yf{tw}")
            for c_ in range(CC):
                ss.inc(sc.activation(sq_bf[:, c_, :], yfb[:, c_, :], AF.Square))
            ss.wait(sc)
            mark(f"ysq{tw}", ss)

        @block.tensor
        def _(t, tw=tw):
            wm(t, f"yf{tw}", f"ysq{tw}")
            for c_ in range(CC):
                ts.inc(t.matmul(pr1[0:1, 0:MW], onespp[:, :], yfb[:, c_, :],
                                start=(c_ == 0), stop=(c_ == CC - 1)))
            for c_ in range(CC):
                ts.inc(t.matmul(pr2[0:1, 0:MW], onespp_b[:, :], sq_bf[:, c_, :],
                                start=(c_ == 0), stop=(c_ == CC - 1)))
            ts.wait(t)
            mark(f"ysum{tw}", ts)

        @block.vector
        def _(v, tw=tw):
            wm(v, f"ysum{tw}")
            vs.inc(v.tensor_scalar(row(RMU), pr1[0:1, 0:MW], 1.0 / D, None, OP.mult))
            vs.inc(v.tensor_scalar(row(RT0), pr2[0:1, 0:MW], 1.0 / D, None, OP.mult))
            vs.wait(v)
            vs.inc(v.tensor_tensor(row(RT1), row(RMU), row(RMU), OP.mult))
            vs.wait(v)
            vs.inc(v.tensor_tensor(row(RT0), row(RT0), row(RT1), OP.subtract))
            vs.wait(v)
            mark(f"yvar{tw}", vs)

        @block.scalar
        def _(sc, tw=tw):
            wm(sc, f"yvar{tw}")
            ss.inc(sc.activation(row(RT1), row(RT0), AF.Sqrt,
                                 bias=consts[0:1, CEPS12:CEPS12 + 1]))
            ss.wait(sc)
            mark(f"ysd{tw}", ss)

        @block.vector
        def _(v, tw=tw):
            wm(v, f"ysd{tw}")
            vs.inc(v.reciprocal(row(RRSTD), row(RT1)))
            vs.wait(v)
            vs.inc(v.tensor_copy(rwb(0), row(RMU)))
            vs.inc(v.tensor_copy(rwb(1), row(RRSTD)))
            vs.wait(v)
            mark(f"yrows{tw}", vs)

        @block.tensor
        def _(t, tw=tw):
            wm(t, f"yrows{tw}")
            ts.inc(t.matmul(pa[:, 0:MW], ones_col[:, :], rwb(0), start=True, stop=True))
            ts.inc(t.matmul(pb[:, 0:MW], ones_col[:, :], rwb(1), start=True, stop=True))
            ts.wait(t)
            mark(f"ybc{tw}", ts)

        @block.vector
        def _(v, tw=tw):
            wm(v, f"ybc{tw}")
            for c_ in range(CC):
                vs.inc(v.tensor_tensor(yfb[:, c_, :], yfb[:, c_, :], pa[:, 0:MW],
                                       OP.subtract))
                vs.wait(v)
                vs.inc(v.tensor_tensor(yfb[:, c_, :], yfb[:, c_, :], pb[:, 0:MW],
                                       OP.mult))
                vs.wait(v)
                vs.inc(v.tensor_scalar(yfb[:, c_, :], yfb[:, c_, :],
                                       cv("gam_moe", c_), cv("bet_moe", c_),
                                       OP.mult, OP.add))
                vs.wait(v)
            mark(f"yln{tw}", vs)

        @block.scalar
        def _(sc, tw=tw):
            wm(sc, f"yln{tw}")
            for c_ in range(CC):
                ss.inc(sc.activation(gtw[:, c_, :], yfb[:, c_, :], AF.Silu))
            ss.wait(sc)
            mark(f"g{tw}", ss)

        @block.sync
        def _(sy, tw=tw, t0=t0):
            ld.wait(sy)
            if tw > 0:
                wm(sy, f"x1w{tw-1}")
            for c_ in range(CC):
                ld.inc(sy.dma_start(out=xtile[:, c_, :],
                                    in_=x_fm.ap()[128 * c_:128 * c_ + 128,
                                                  t0:t0 + MW]), 16)
            mark(f"xre{tw}", ld)

        @block.vector
        def _(v, tw=tw, t0=t0):
            wm(v, f"g{tw}", f"xre{tw}")
            for c_ in range(CC):
                vs.inc(v.tensor_tensor(gtw[:, c_, :], sg_bf[:, c_, t0:t0 + MW],
                                       gtw[:, c_, :], OP.mult))
                vs.wait(v)
                vs.inc(v.tensor_tensor(xtile[:, c_, :], xtile[:, c_, :],
                                       gtw[:, c_, :], OP.add))
                vs.wait(v)
            mark(f"x1{tw}", vs)

        @block.sync
        def _(sy, tw=tw, t0=t0):
            ld.wait(sy)
            wm(sy, f"x1{tw}")
            for c_ in range(CC):
                ld.inc(sy.dma_start(out=cc_in.ap()[128 * c_:128 * c_ + 128,
                                                   t0:t0 + MW],
                                    in_=xtile[:, c_, :]), 16)
            ld.wait(sy)
            ys.inc(sy.nop())
            mark(f"x1w{tw}", ys)

        @block.vector
        def _(v, tw=tw):
            wm(v, f"x1{tw}")
            vs.inc(v.nop())
            vs.wait(v)
            mark(f"gdone{tw}", vs)

    barrier()

    @block.gpsimd
    def _(g):
        ccs.inc(g.collective_compute(
            "AllReduce" if False else "AllGather", OP.bypass,
            replica_groups=[list(range(NCORES))],
            ins=[cc_in.ap().opt()], outs=[x1all.ap().opt()]))

    ar.free("maskspad", "xtile", "sq_bf", "wbufA1", "wbufA2", "wbufB",
            "xm_bf", "inter", "yfb", "gtw", "b1pad")
    wq_f = A("wq_f", [128, 4096], F32, hi_side=True)
    wq_b = A("wq_b", [128, 4096], BF16, hi_side=True)

    for wi, (wsrc, wdst, nko, kwid) in enumerate(
            [(w1l1_t, w1qd, IC, 1024), (w2l1_t, w2qd, CC, 4096)]):
        for oc in range(nko):
            @block.sync
            def _(sy, wi=wi, wsrc=wsrc, oc=oc, kwid=kwid):
                ld.wait(sy)
                if oc > 0:
                    wm(sy, f"wabs{wi}_{oc-1}")
                elif wi > 0:
                    wm(sy, f"wqs{wi-1}_{IC-1}")
                ld.inc(sy.dma_start(out=wq_f[:, 0:kwid], in_=wsrc.ap()[oc]), 16)
                mark(f"wpl{wi}_{oc}", ld)

            @block.vector
            def _(v, wi=wi, oc=oc, kwid=kwid, nko=nko):
                wm(v, f"wpl{wi}_{oc}")
                vs.inc(v.tensor_reduce(colbuf[:, 16:17], wq_f[:, 0:kwid], AX.X,
                                       OP.add, apply_absolute_value=True))
                vs.wait(v)
                if oc == 0:
                    vs.inc(v.tensor_copy(colbuf[:, 17:18], colbuf[:, 16:17]))
                else:
                    vs.inc(v.tensor_tensor(colbuf[:, 17:18], colbuf[:, 17:18],
                                           colbuf[:, 16:17], OP.add))
                vs.wait(v)
                mark(f"wabs{wi}_{oc}", vs)
                if oc == nko - 1:
                    mark(f"wabs{wi}_done", vs)

        @block.tensor
        def _(t, wi=wi):
            wm(t, f"wabs{wi}_done")
            if wi > 0:
                wm(t, f"swpp{wi-1}")
            ts.inc(t.matmul(pr1[0:1, 0:1], colbuf[:, 17:18], onespp[:, 0:1],
                            start=True, stop=True))
            ts.wait(t)
            mark(f"wtot{wi}", ts)

        @block.vector
        def _(v, wi=wi, nko=nko, kwid=kwid):
            wm(v, f"wtot{wi}")
            nelem = float(nko * 128 * kwid)
            vs.inc(v.tensor_scalar(row(RW0, 1), pr1[0:1, 0:1], 1.0 / nelem, None,
                                   OP.mult))
            vs.wait(v)
            vs.inc(v.tensor_scalar(row(RW0, 1), row(RW0, 1), 1e-5, None, OP.max))
            vs.wait(v)
            vs.inc(v.reciprocal(row(RW1, 1), row(RW0, 1)))
            vs.wait(v)
            vs.inc(v.tensor_copy(rwb(2, 1), row(RW1, 1)))
            vs.wait(v)
            mark(f"swrow{wi}", vs)

        @block.tensor
        def _(t, wi=wi):
            wm(t, f"swrow{wi}")
            ts.inc(t.matmul(po[:, 16 + wi:17 + wi], ones_col[:, :], rwb(2, 1),
                            start=True, stop=True))
            ts.wait(t)
            mark(f"swbc{wi}", ts)

        @block.vector
        def _(v, wi=wi):
            wm(v, f"swbc{wi}")
            vs.inc(v.tensor_copy(colbuf[:, 18 + wi:19 + wi], po[:, 16 + wi:17 + wi]))
            vs.wait(v)
            vs.inc(v.reciprocal(colbuf[:, 20 + wi:21 + wi],
                                colbuf[:, 18 + wi:19 + wi]))
            vs.wait(v)
            mark(f"swpp{wi}", vs)

        for oc in range(nko):
            @block.sync
            def _(sy, wi=wi, wsrc=wsrc, oc=oc, kwid=kwid):
                wm(sy, f"wqs{wi}_{oc-1}" if oc > 0 else f"swpp{wi}")
                ld.inc(sy.dma_start(out=wq_f[:, 0:kwid], in_=wsrc.ap()[oc]), 16)
                mark(f"wp2{wi}_{oc}", ld)

            @block.vector
            def _(v, wi=wi, oc=oc, kwid=kwid):
                wm(v, f"wp2{wi}_{oc}")
                vs.inc(v.tensor_scalar(wq_f[:, 0:kwid], wq_f[:, 0:kwid],
                                       colbuf[:, 18 + wi:19 + wi], RC,
                                       OP.mult, OP.add))
                vs.wait(v)
                vs.inc(v.tensor_scalar(wq_f[:, 0:kwid], wq_f[:, 0:kwid],
                                       RC, 1.0, OP.subtract, OP.min))
                vs.wait(v)
                vs.inc(v.tensor_scalar(wq_b[:, 0:kwid], wq_f[:, 0:kwid],
                                       -1.0, None, OP.max))
                vs.wait(v)
                mark(f"wq{wi}_{oc}", vs)

            @block.gpsimd
            def _(g, wi=wi, wdst=wdst, oc=oc, kwid=kwid):
                wm(g, f"wq{wi}_{oc}")
                xd.inc(g.dma_start(out=wdst.ap()[oc], in_=wq_b[:, 0:kwid]), 16)
                xd.wait(g)
                gs.inc(g.nop())
                mark(f"wqs{wi}_{oc}", gs)

    @block.gpsimd
    def _(g):
        ccs.wait(g)
        gs.inc(g.nop())
        mark("ag_done", gs)

    barrier()
    ar.free("wq_f", "wq_b", "xn_bf", "onehot8", "sg_bf", "rows",
            "ix_u32", "mt_bf")

    if PHASE_C:
        xn1_bf = A("xn1_bf", [128, CC, S], BF16)
        xsl = A("xsl", [128, S], F32, hi_side=True)
        xpb = A("xpb", [128, S], F32, hi_side=True)
        c1sq = A("c1sq", [128, S], BF16, hi_side=True)
        maxfold = A("maxfold", [128, S], F32)
        rowsl = A("rowsl", [1, 6 * S], F32)
        rowslb = A("rowslb", [1, 2 * S], BF16)
        wwin = A("wwin", [128, MW], F32, hi_side=True)
        xq1w_bf = A("xq1w_bf", [128, MW], BF16, hi_side=True)

        def rl(i, n=S):
            return rowsl[:, S * i:S * i + n]

        def rlb(i, n=S):
            return rowslb[:, S * i:S * i + n]

        RLMU, RLRSTD, RLT0, RLT1, RLRMS, RLIV = 0, 1, 2, 3, 4, 5

        SPART = S // 512
        part_banks = [pa, pb, pc, pd]

        for c_ in range(CC):
            @block.gpsimd
            def _(g, c_=c_):
                if c_ == 0:
                    wm(g, "ag_done")
                else:
                    wm(g, f"c1xpb{c_-1}")
                pid = g.partition_id()
                off = pid * W + c_ * 128 * S
                src = AP(tensor=x1all.ap().tensor, offset=off,
                         ap=[[S, 128], [D * S, B], [1, W]])
                xd.inc(g.dma_start(out=xsl[:, :], in_=src), 16)
                xd.wait(g)
                gs.inc(g.nop())
                mark(f"c1xsl{c_}", gs)

            @block.vector
            def _(v, c_=c_):
                wm(v, f"c1xsl{c_}")
                if c_ > 0:
                    wm(v, f"c1mm{c_-1}", f"c1sav{c_-1}")
                vs.inc(v.tensor_copy(
                    xpb[:, :].rearrange("q (p b) -> q b p", b=B),
                    xsl[:, :].rearrange("q (b p) -> q b p", b=B)))
                vs.wait(v)
                mark(f"c1xpb{c_}", vs)

            @block.gpsimd
            def _(g, c_=c_):
                wm(g, f"c1xpb{c_}")
                xd.inc(g.dma_start(out=x1pbd.ap()[128 * c_:128 * c_ + 128, :],
                                   in_=xpb[:, :]), 16)
                xd.wait(g)
                gs.inc(g.nop())
                mark(f"c1sav{c_}", gs)

            @block.scalar
            def _(sc, c_=c_):
                wm(sc, f"c1xpb{c_}")
                if c_ > 0:
                    wm(sc, f"c1mm{c_-1}")
                ss.inc(sc.activation(c1sq[:, :], xpb[:, :], AF.Square))
                ss.wait(sc)
                mark(f"c1sq{c_}", ss)

            @block.tensor
            def _(t, c_=c_):
                wm(t, f"c1xpb{c_}", f"c1sq{c_}")
                for part in range(SPART):
                    bank = part_banks[part]
                    ts.inc(t.matmul(bank[0:1, 0:512], onespp[:, :],
                                    xpb[:, 512 * part:512 * part + 512],
                                    start=(c_ == 0), stop=(c_ == CC - 1)))
                    ts.inc(t.matmul(bank[32:33, 0:512], onespp_b[:, :],
                                    c1sq[:, 512 * part:512 * part + 512],
                                    start=(c_ == 0), stop=(c_ == CC - 1)))
                ts.wait(t)
                mark(f"c1mm{c_}", ts)

        @block.vector
        def _(v):
            wm(v, f"c1mm{CC-1}")
            for part in range(SPART):
                bank = part_banks[part]
                sl = slice(512 * part, 512 * part + 512)
                vs.inc(v.tensor_scalar(rl(RLMU)[:, sl], bank[0:1, 0:512],
                                       1.0 / D, None, OP.mult))
                vs.inc(v.tensor_scalar(rl(RLT0)[:, sl], bank[32:33, 0:512],
                                       1.0 / D, None, OP.mult))
            vs.wait(v)
            vs.inc(v.tensor_tensor(rl(RLT1), rl(RLMU), rl(RLMU), OP.mult))
            vs.wait(v)
            vs.inc(v.tensor_tensor(rl(RLT0), rl(RLT0), rl(RLT1), OP.subtract))
            vs.wait(v)
            mark("c1var", vs)

        @block.scalar
        def _(sc):
            wm(sc, "c1var")
            ss.inc(sc.activation(rl(RLT1), rl(RLT0), AF.Sqrt,
                                 bias=consts[0:1, CEPS5:CEPS5 + 1]))
            ss.wait(sc)
            mark("c1sd", ss)

        @block.vector
        def _(v):
            wm(v, "c1sd")
            vs.inc(v.reciprocal(rl(RLRSTD), rl(RLT1)))
            vs.wait(v)
            vs.inc(v.tensor_tensor(rl(RLT1), rl(RLRSTD), rl(RLRSTD), OP.mult))
            vs.wait(v)
            vs.inc(v.tensor_tensor(rl(RLT1), rl(RLT0), rl(RLT1), OP.mult))
            vs.wait(v)
            mark("c1m2", vs)

        @block.scalar
        def _(sc):
            wm(sc, "c1m2")
            ss.inc(sc.activation(rl(RLT1), rl(RLT1), AF.Sqrt,
                                 bias=consts[0:1, CEPS6:CEPS6 + 1]))
            ss.wait(sc)
            mark("c1rmsd", ss)

        @block.vector
        def _(v):
            wm(v, "c1rmsd")
            vs.inc(v.reciprocal(rl(RLRMS), rl(RLT1)))
            vs.wait(v)
            vs.inc(v.tensor_copy(rlb(0), rl(RLMU)))
            vs.inc(v.tensor_copy(rlb(1), rl(RLRSTD)))
            vs.wait(v)
            vs.inc(v.memset(maxfold[:, :], 0.0))
            vs.wait(v)
            mark("c1rows", vs)

        for mw in range(NMW):
            m0 = MW * mw

            @block.tensor
            def _(t, mw=mw, m0=m0):
                wm(t, "c1rows")
                if mw > 0:
                    wm(t, f"c1xn{mw-1}_{CC-1}")
                ts.inc(t.matmul(pa[:, 0:MW], ones_col[:, :], rlb(0)[:, m0:m0 + MW],
                                start=True, stop=True))
                ts.inc(t.matmul(pb[:, 0:MW], ones_col[:, :], rlb(1)[:, m0:m0 + MW],
                                start=True, stop=True))
                ts.wait(t)
                mark(f"c1bc{mw}", ts)

            for c_ in range(CC):
                @block.sync
                def _(sy, mw=mw, m0=m0, c_=c_):
                    ld.wait(sy)
                    wm(sy, f"c1sav{c_}")
                    if mw + c_ > 0:
                        pcc = (mw, c_ - 1) if c_ > 0 else (mw - 1, CC - 1)
                        wm(sy, f"c1xn{pcc[0]}_{pcc[1]}")
                    ld.inc(sy.dma_start(out=wwin[:, :],
                                        in_=x1pbd.ap()[128 * c_:128 * c_ + 128,
                                                       m0:m0 + MW]), 16)
                    mark(f"c1wl{mw}_{c_}", ld)

                @block.vector
                def _(v, mw=mw, m0=m0, c_=c_):
                    wm(v, f"c1bc{mw}", f"c1wl{mw}_{c_}")
                    vs.inc(v.tensor_tensor(wwin[:, :], wwin[:, :], pa[:, 0:MW],
                                           OP.subtract))
                    vs.wait(v)
                    vs.inc(v.tensor_tensor(xn1_bf[:, c_, m0:m0 + MW], wwin[:, :],
                                           pb[:, 0:MW], OP.mult))
                    vs.wait(v)
                    vs.inc(v.tensor_tensor(maxfold[:, m0:m0 + MW],
                                           maxfold[:, m0:m0 + MW],
                                           xn1_bf[:, c_, m0:m0 + MW], OP.max))
                    vs.wait(v)
                    vs.inc(v.tensor_scalar(wwin[:, :], xn1_bf[:, c_, m0:m0 + MW],
                                           -1.0, None, OP.mult))
                    vs.wait(v)
                    vs.inc(v.tensor_tensor(maxfold[:, m0:m0 + MW],
                                           maxfold[:, m0:m0 + MW],
                                           wwin[:, :], OP.max))
                    vs.wait(v)
                    mark(f"c1xn{mw}_{c_}", vs)

        for tt in range(16):
            @block.tensor
            def _(t, tt=tt):
                wm(t, f"c1xn{NMW-1}_{CC-1}")
                if tt > 0:
                    wm(t, f"c1mc{tt-1}")
                ts.inc(t.transpose(po[:, :], maxfold[:, 128 * tt:128 * tt + 128],
                                   idf[:, :]))
                ts.wait(t)
                mark(f"c1mt{tt}", ts)

            @block.vector
            def _(v, tt=tt):
                wm(v, f"c1mt{tt}")
                vs.inc(v.tensor_reduce(colbuf[:, 16:17], po[:, :], AX.X, OP.max))
                vs.wait(v)
                mark(f"c1mc{tt}", vs)

            @block.tensor
            def _(t, tt=tt):
                wm(t, f"c1mc{tt}")
                if tt >= 4:
                    wm(t, f"c1mr{tt // 4 - 1}")
                ts.inc(t.matmul(pr1[0:1, 128 * (tt % 4):128 * (tt % 4) + 128],
                                colbuf[:, 16:17], idf[:, :],
                                start=True, stop=True))
                ts.wait(t)
                mark(f"c1fl{tt}", ts)

            if tt % 4 == 3:
                @block.vector
                def _(v, tt=tt):
                    wm(v, f"c1fl{tt}")
                    q = tt // 4
                    vs.inc(v.tensor_copy(rl(RLT0)[:, 512 * q:512 * q + 512],
                                         pr1[0:1, 0:512]))
                    vs.wait(v)
                    mark(f"c1mr{q}", vs)

        @block.vector
        def _(v):
            wm(v, "c1mr3")
            vs.inc(v.tensor_tensor(rl(RLT0), rl(RLT0), rl(RLRMS), OP.mult))
            vs.wait(v)
            vs.inc(v.tensor_scalar(rl(RLT0), rl(RLT0), 1e-5, None, OP.max))
            vs.wait(v)
            vs.inc(v.reciprocal(rl(RLT1), rl(RLT0)))
            vs.wait(v)
            vs.inc(v.tensor_scalar(rl(RLT1), rl(RLT1), 127.0, None, OP.mult))
            vs.wait(v)
            vs.inc(v.tensor_tensor(rl(RLT1), rl(RLT1), rl(RLRMS), OP.mult))
            vs.wait(v)
            vs.inc(v.tensor_copy(rlb(0), rl(RLT1)))
            vs.inc(v.tensor_scalar(rl(RLIV), rl(RLT0), 1.0 / 127.0, None, OP.mult))
            vs.wait(v)
            vs.inc(v.tensor_single_scalar(rl(RLIV), rl(RLIV),
                                          colbuf[0:1, 20:21], OP.mult))
            vs.wait(v)
            vs.inc(v.tensor_copy(rlb(1), rl(RLIV)))
            vs.wait(v)
            mark("c1scale", vs)

        for mw in range(NMW):
            m0 = MW * mw

            @block.tensor
            def _(t, mw=mw, m0=m0):
                wm(t, "c1scale")
                if mw > 0:
                    wm(t, f"c1qs{mw-1}_{CC-1}")
                ts.inc(t.matmul(pa[:, 0:MW], ones_col[:, :], rlb(0)[:, m0:m0 + MW],
                                start=True, stop=True))
                ts.wait(t)
                mark(f"c1qb{mw}", ts)

            for c_ in range(CC):
                @block.vector
                def _(v, mw=mw, m0=m0, c_=c_):
                    wm(v, f"c1qb{mw}")
                    if mw + c_ > 0:
                        pcc = (mw, c_ - 1) if c_ > 0 else (mw - 1, CC - 1)
                        wm(v, f"c1qsp{pcc[0]}_{pcc[1]}")
                    vs.inc(v.tensor_tensor(wwin[:, :], xn1_bf[:, c_, m0:m0 + MW],
                                           pa[:, 0:MW], OP.mult))
                    vs.wait(v)
                    vs.inc(v.tensor_scalar(wwin[:, :], wwin[:, :], RC, RC,
                                           OP.add, OP.subtract))
                    vs.wait(v)
                    vs.inc(v.tensor_copy(xq1w_bf[:, :], wwin[:, :]))
                    vs.wait(v)
                    mark(f"c1qs{mw}_{c_}", vs)

                @block.gpsimd
                def _(g, mw=mw, m0=m0, c_=c_):
                    wm(g, f"c1qs{mw}_{c_}")
                    xd.inc(g.dma_start(out=xq1d.ap()[128 * c_:128 * c_ + 128,
                                                     m0:m0 + MW],
                                       in_=xq1w_bf[:, :]), 16)
                    xd.wait(g)
                    gs.inc(g.nop())
                    mark(f"c1qsp{mw}_{c_}", gs)

        barrier()
        ar.free("xsl", "xpb", "c1sq", "maxfold", "wwin", "xq1w_bf", "rowsl")

        qh_fm = A("qh_fm", [128, CC, S], BF16)
        kh_fm = A("kh_fm", [128, CC, S], BF16)
        vw_lo = A("vw_lo", [128, 8, 1024], BF16)
        vw_hi = A("vw_hi", [128, 8, 1024], BF16)

        def vw_tc(tc):
            return (vw_lo if tc < 8 else vw_hi)[:, tc % 8, :]
        sigwr = A("sigwr", [128, 16, 32], F32)
        biaspad1 = A("biaspad1", [128, 3104], BF16, hi_side=True)
        wl1g = A("wl1g", [128, CC, 1024], BF16, hi_side=True)
        qtm = A("qtm", [128, 1024], BF16, hi_side=True)
        sq2w = A("sq2w", [128, 1024], BF16, hi_side=True)
        rn16 = A("rn16", [128, 16], F32)

        @block.gpsimd
        def _(g):
            gs.inc(g.memset(biaspad1[:, :], 0.0))
            gs.wait(g)
            gd.inc(g.dma_start(out=biaspad1[0:1, :], in_=bl1.ap()[:, :]), 16)
            gd.wait(g)
            gs.inc(g.nop())
            mark("c2bias", gs)

        GROUPS = [("wr", 3072, 32), ("q", 0, 1024), ("k", 1024, 1024),
                  ("v", 2048, 1024)]
        for gi_, (gnm, col0, width) in enumerate(GROUPS):
            @block.gpsimd
            def _(g, gi_=gi_, col0=col0, width=width):
                if gi_ > 0:
                    wm(g, f"c2mm{gi_-1}_{15}")
                for k in range(CC):
                    gd.inc(g.dma_start(out=wl1g[:, k, 0:width],
                                       in_=wl1_T.ap()[k][:, col0:col0 + width]), 16)
                gd.wait(g)
                gs.inc(g.nop())
                mark(f"c2w{gi_}", gs)

            for tc in range(16):
                @block.tensor
                def _(t, gi_=gi_, gnm=gnm, col0=col0, width=width, tc=tc):
                    wm(t, f"c2w{gi_}", "c2bias")
                    if gi_ + tc > 0:
                        ptc = (gi_, tc - 1) if tc > 0 else (gi_ - 1, 15)
                        wm(t, f"c2ev{ptc[0]}_{ptc[1]}")
                    nwin = (width + 511) // 512
                    for win in range(nwin):
                        wd = min(512, width - 512 * win)
                        pbuf = pa if win == 0 else pb
                        for k in range(CC):
                            ts.inc(t.matmul(
                                pbuf[:, 0:wd],
                                xn1_bf[:, k, 128 * tc:128 * tc + 128],
                                wl1g[:, k, 512 * win:512 * win + wd],
                                start=(k == 0), stop=False))
                        ts.inc(t.matmul(
                            pbuf[:, 0:wd], ones128[:, :],
                            biaspad1[:, col0 + 512 * win:col0 + 512 * win + wd],
                            start=False, stop=True))
                    ts.wait(t)
                    mark(f"c2mm{gi_}_{tc}", ts)

                if gnm == "wr":
                    @block.scalar
                    def _(sc, gi_=gi_, tc=tc):
                        wm(sc, f"c2mm{gi_}_{tc}")
                        ss.inc(sc.activation(sigwr[:, tc, :], pa[:, 0:32],
                                             AF.Sigmoid))
                        ss.wait(sc)
                        mark(f"c2ev{gi_}_{tc}", ss)
                elif gnm in ("q", "k"):
                    dst = None

                    @block.scalar
                    def _(sc, gi_=gi_, tc=tc):
                        wm(sc, f"c2mm{gi_}_{tc}")
                        ss.inc(sc.activation(sq2w[:, 0:512], pa[:, 0:512], AF.Square))
                        ss.inc(sc.activation(sq2w[:, 512:1024], pb[:, 0:512],
                                             AF.Square))
                        ss.wait(sc)
                        mark(f"c2sq{gi_}_{tc}", ss)

                    @block.vector
                    def _(v, gi_=gi_, gnm=gnm, tc=tc):
                        wm(v, f"c2sq{gi_}_{tc}")
                        vs.inc(v.tensor_reduce(
                            rn16[:, :], sq2w[:, :].rearrange("p (n h) -> p n h", n=16),
                            AX.X, OP.add))
                        vs.wait(v)
                        mark(f"c2nr{gi_}_{tc}", vs)

                    @block.scalar
                    def _(sc, gi_=gi_, tc=tc):
                        wm(sc, f"c2nr{gi_}_{tc}")
                        ss.inc(sc.activation(rn16[:, :], rn16[:, :], AF.Sqrt))
                        ss.wait(sc)
                        mark(f"c2nsq{gi_}_{tc}", ss)

                    @block.vector
                    def _(v, gi_=gi_, gnm=gnm, tc=tc):
                        wm(v, f"c2nsq{gi_}_{tc}")
                        vs.inc(v.tensor_scalar(rn16[:, :], rn16[:, :], 1e-12, None,
                                               OP.max))
                        vs.wait(v)
                        vs.inc(v.reciprocal(rn16[:, :], rn16[:, :]))
                        vs.wait(v)
                        for half_ in range(2):
                            pbuf = pa if half_ == 0 else pb
                            vs.inc(v.tensor_tensor(
                                qtm[:, 512 * half_:512 * half_ + 512]
                                .rearrange("p (n h) -> p n h", n=8),
                                pbuf[:, 0:512].rearrange("p (n h) -> p n h", n=8),
                                rn16[:, 8 * half_:8 * half_ + 8]
                                .unsqueeze(2).broadcast_to([128, 8, 64]),
                                OP.mult))
                        vs.wait(v)
                        mark(f"c2qn{gi_}_{tc}", vs)

                    for fc in range(CC):
                        @block.tensor
                        def _(t, gi_=gi_, tc=tc, fc=fc):
                            wm(t, f"c2qn{gi_}_{tc}")
                            if fc > 0 or tc > 0 or gi_ > 1:
                                wm(t, "c2tr_prev")
                            ts.inc(t.transpose(ptb[:, 0:128],
                                               qtm[:, 128 * fc:128 * fc + 128],
                                               idv[:, :]))
                            ts.wait(t)
                            mark("c2tr_cur", ts)

                        @block.vector
                        def _(v, gi_=gi_, gnm=gnm, tc=tc, fc=fc):
                            wm(v, "c2tr_cur")
                            dstf = qh_fm if gnm == "q" else kh_fm
                            vs.inc(v.tensor_copy(
                                dstf[:, fc, 128 * tc:128 * tc + 128],
                                ptb[:, 0:128]))
                            vs.wait(v)
                            mark("c2tr_prev", vs)
                    if tc == 15:
                        @block.vector
                        def _(v, gi_=gi_):
                            vs.inc(v.nop())
                            vs.wait(v)
                            mark(f"c2ev{gi_}_{15}", vs)
                    else:
                        @block.vector
                        def _(v, gi_=gi_, tc=tc):
                            vs.inc(v.nop())
                            vs.wait(v)
                            mark(f"c2ev{gi_}_{tc}", vs)
                else:
                    @block.vector
                    def _(v, gi_=gi_, tc=tc):
                        wm(v, f"c2mm{gi_}_{tc}")
                        for half_ in range(2):
                            pbuf = pa if half_ == 0 else pb
                            vs.inc(v.tensor_tensor(
                                vw_tc(tc)[:, 512 * half_:512 * half_ + 512]
                                .rearrange("p (n h) -> p n h", n=8),
                                pbuf[:, 0:512].rearrange("p (n h) -> p n h", n=8),
                                sigwr[:, tc, 8 * half_:8 * half_ + 8]
                                .unsqueeze(2).broadcast_to([128, 8, 64]),
                                OP.mult))
                        vs.wait(v)
                        mark(f"c2ev{gi_}_{tc}", vs)

        barrier()
        ar.free("biaspad1", "wl1g", "qtm", "sq2w")

        oat_tm = A("oat_tm", [128, 16, 1024], BF16)
        scb = A("scb", [128, 2, 128], BF16, hi_side=True)

        it = 0
        for n in range(16):
            fc = n // 2
            r0 = 64 * (n % 2)
            for tc in range(16):
                pS = po if it % 2 == 0 else pd
                pO = pa if it % 2 == 0 else pb

                @block.tensor
                def _(t, n=n, fc=fc, r0=r0, tc=tc, it=it, pS=pS):
                    if it >= 2:
                        wm(t, f"c3sc{it-2}")
                    ts.inc(t.matmul(pS[:, 0:128] if pS is po else pS[:, 0:128],
                                    kh_fm[r0:r0 + 64, fc, 128 * tc:128 * tc + 128],
                                    qh_fm[r0:r0 + 64, fc, 128 * tc:128 * tc + 128],
                                    start=True, stop=True))
                    mark(f"c3s{it}", ts)

                @block.vector
                def _(v, it=it, pS=pS):
                    wm(v, f"c3s{it}")
                    vs.inc(v.tensor_tensor(scb[:, it % 2, :],
                                           pS[:, 0:128] if pS is po else pS[:, 0:128],
                                           bdmask[:, :], OP.mult))
                    vs.wait(v)
                    mark(f"c3sc{it}", vs)

                @block.tensor
                def _(t, n=n, tc=tc, it=it, pO=pO):
                    wm(t, f"c3sc{it}")
                    if it >= 2:
                        wm(t, f"c3oe{it-2}")
                    ts.inc(t.matmul(pO[:, 0:64], scb[:, it % 2, :],
                                    vw_tc(tc)[:, 64 * n:64 * n + 64],
                                    start=True, stop=True))
                    mark(f"c3o{it}", ts)

                @block.vector
                def _(v, n=n, tc=tc, it=it, pO=pO):
                    wm(v, f"c3o{it}")
                    vs.inc(v.tensor_scalar(oat_tm[:, tc, 64 * n:64 * n + 64],
                                           pO[:, 0:64],
                                           sigwr[:, tc, 16 + n:17 + n], None,
                                           OP.mult))
                    vs.wait(v)
                    mark(f"c3oe{it}", vs)
                it += 1

        barrier()
        ar.free("qh_fm", "kh_fm", "vw_lo", "vw_hi", "scb", "rn16")

        oat_fm = A("oat_fm", [128, CC, S], BF16)
        sg_bf = A("sg_bf", [128, CC, S], BF16)
        wodb = A("wodb", [128, 2, 1024], BF16, hi_side=True)

        it = 0
        for tc in range(16):
            for fc in range(CC):
                @block.tensor
                def _(t, tc=tc, fc=fc, it=it):
                    if it > 0:
                        wm(t, f"c4te{it-1}")
                    ts.inc(t.transpose(ptb[:, 0:128],
                                       oat_tm[:, tc, 128 * fc:128 * fc + 128],
                                       idv[:, :]))
                    ts.wait(t)
                    mark(f"c4tr{it}", ts)

                @block.vector
                def _(v, tc=tc, fc=fc, it=it):
                    wm(v, f"c4tr{it}")
                    vs.inc(v.tensor_copy(oat_fm[:, fc, 128 * tc:128 * tc + 128],
                                         ptb[:, 0:128]))
                    vs.wait(v)
                    mark(f"c4te{it}", vs)
                it += 1

        for oc in range(CC):
            @block.gpsimd
            def _(g, oc=oc):
                if oc >= 2:
                    wm(g, f"c4mm{oc-2}_{3}")
                gd.inc(g.dma_start(out=wodb[:, oc % 2, :], in_=wo1_t.ap()[oc]), 16)
                gd.wait(g)
                gs.inc(g.nop())
                mark(f"c4wl{oc}", gs)

            for twn in range(4):
                i = oc * 4 + twn

                @block.tensor
                def _(t, oc=oc, twn=twn, i=i):
                    wm(t, f"c4wl{oc}")
                    if twn == 0:
                        wm(t, f"c4te{127}")
                    pbuf = pa if i % 2 == 0 else pb
                    if i >= 2:
                        wm(t, f"c4ev{i-2}")
                    for k in range(CC):
                        ts.inc(t.matmul(pbuf[:, :],
                                        wodb[:, oc % 2, 128 * k:128 * k + 128],
                                        oat_fm[:, k, 512 * twn:512 * twn + 512],
                                        start=(k == 0), stop=(k == CC - 1)))
                    ts.wait(t)
                    mark(f"c4mm{oc}_{twn}", ts)

                @block.vector
                def _(v, oc=oc, twn=twn, i=i):
                    wm(v, f"c4mm{oc}_{twn}")
                    pbuf = pa if i % 2 == 0 else pb
                    vs.inc(v.tensor_scalar(sg_bf[:, oc, 512 * twn:512 * twn + 512],
                                           pbuf[:, :], cv("wo1_b", oc), None,
                                           OP.add))
                    vs.wait(v)
                    mark(f"c4ev{i}", vs)

        barrier()
        ar.free("oat_tm", "oat_fm", "wodb", "xn1_bf")

        hbuf = A("hbuf", [128, IC, MW], BF16)
        hq = A("hq", [128, IC, MW], BF16)
        x1w = A("x1w", [128, CC, MW], F32)
        gcw = A("gcw", [128, CC, MW], BF16)
        outT = A("outT", [128, 1024], F32, hi_side=True)
        ivb = A("ivb", [128, MW], F32, hi_side=True)
        qb2 = A("qb2", [128, MW], F32, hi_side=True)
        sq2c = A("sq2c", [128, MW], BF16, hi_side=True)
        sq2d = A("sq2d", [128, MW], BF16, hi_side=True)
        mf2 = A("mf2", [128, MW], F32, hi_side=True)
        mf2r = A("mf2r", [128, 8, MW], F32, hi_side=True)
        mxq = A("mxq", [128, MW], F32, hi_side=True)
        negc = A("negc", [128, 1], F32, hi_side=True)
        xq1w = A("xq1w", [128, CC, MW], BF16, hi_side=True)
        w1db = A("w1db", [128, 2, 1024], BF16, hi_side=True)
        w2db = A("w2db", [128, 2, 4096], BF16, hi_side=True)
        rw2 = A("rw2", [1, 4 * MW], F32, hi_side=True)

        def r2(i):
            return rw2[:, MW * i:MW * i + MW]

        @block.vector
        def _(v):
            vs.inc(v.memset(negc[:, :], -RC))
            vs.wait(v)
            mark("c5negc", vs)

        for cw in range(NMW):
            m0 = MW * cw

            @block.sync
            def _(sy, cw=cw, m0=m0):
                if cw > 0:
                    wm(sy, f"c5w1mm{cw-1}_{IC-1}")
                ld.wait(sy)
                for c_ in range(CC):
                    ld.inc(sy.dma_start(out=xq1w[:, c_, :],
                                        in_=xq1d.ap()[128 * c_:128 * c_ + 128,
                                                      m0:m0 + MW]), 16)
                mark(f"c5xq{cw}", ld)

            @block.tensor
            def _(t, cw=cw, m0=m0):
                wm(t, "c1scale")
                if cw > 0:
                    wm(t, f"c5ivb{cw-1}", f"c5iv2{cw-1}")
                ts.inc(t.matmul(pc[:, 0:MW], ones_col[:, :], rlb(1)[:, m0:m0 + MW],
                                start=True, stop=True))
                ts.wait(t)
                mark(f"c5iv{cw}", ts)

            @block.vector
            def _(v, cw=cw):
                wm(v, f"c5iv{cw}")
                vs.inc(v.tensor_copy(ivb[:, :], pc[:, 0:MW]))
                vs.wait(v)
                mark(f"c5ivb{cw}", vs)

            def q2_sections(cw, ocq, m0):
                sq = sq2c if ocq % 2 == 0 else sq2d

                @block.vector
                def _(v, cw=cw, ocq=ocq, sq=sq):
                    wm(v, f"c5h{cw}_{ocq}")
                    idx = cw * IC + ocq
                    if idx >= 2:
                        pi = idx - 2
                        wm(v, f"c5q2m{pi // IC}_{pi % IC}")
                    vs.inc(v.tensor_tensor(sq[:, :], hbuf[:, ocq, :],
                                           hbuf[:, ocq, :], OP.mult))
                    vs.wait(v)
                    mark(f"c5q2s{cw}_{ocq}", vs)

                @block.tensor
                def _(t, cw=cw, ocq=ocq, sq=sq):
                    wm(t, f"c5q2s{cw}_{ocq}")
                    if ocq == 0 and cw > 0:
                        wm(t, f"c5q2r{cw-1}")
                    ts.inc(t.matmul(pr1[0:1, 0:MW], onespp_b[:, :], sq[:, :],
                                    start=(ocq == 0), stop=(ocq == IC - 1)))
                    mark(f"c5q2m{cw}_{ocq}", ts)

                @block.vector
                def _(v, cw=cw, ocq=ocq, sq=sq):
                    wm(v, f"c5q2s{cw}_{ocq}")
                    if ocq == 0:
                        if cw > 0:
                            wm(v, f"c5mt{cw-1}_{1}")
                        vs.inc(v.tensor_copy(mxq[:, :], sq[:, :]))
                    else:
                        vs.inc(v.tensor_tensor(mxq[:, :], mxq[:, :], sq[:, :],
                                               OP.max))
                    vs.wait(v)
                    mark(f"c5q2f{cw}_{ocq}", vs)

            for g in range(IC // 8):
                for k in range(8):
                    oc = 8 * g + k

                    @block.sync
                    def _(sy, cw=cw, oc=oc):
                        idx = cw * IC + oc
                        if idx >= 2:
                            pi = idx - 2
                            wm(sy, f"c5w1mm{pi // IC}_{pi % IC}")
                        wc = wla if oc % 2 == 0 else wlb
                        wc.wait(sy)
                        wc.inc(sy.dma_start(out=w1db[:, oc % 2, :],
                                            in_=w1qd.ap()[oc]), 16)
                        mark(f"c5w1l{cw}_{oc}", wc)

                    @block.tensor
                    def _(t, cw=cw, oc=oc):
                        wm(t, f"c5xq{cw}", f"c5w1l{cw}_{oc}")
                        pbuf = pa if oc % 2 == 0 else pb
                        idx = cw * IC + oc
                        if idx >= 2:
                            pi = idx - 2
                            wm(t, f"c5sc{pi // IC}_{pi % IC}")
                        for k_ in range(CC):
                            ts.inc(t.matmul(pbuf[:, 0:MW],
                                            w1db[:, oc % 2,
                                                 128 * k_:128 * k_ + 128],
                                            xq1w[:, k_, :],
                                            start=(k_ == 0), stop=(k_ == CC - 1)))
                        mark(f"c5w1mm{cw}_{oc}", ts)

                    @block.vector
                    def _(v, cw=cw, oc=oc):
                        wm(v, f"c5w1mm{cw}_{oc}", f"c5ivb{cw}")
                        gi = oc // 8
                        if gi > 0:
                            wm(v, f"c5hg{cw}_{gi-1}")
                        elif cw > 0:
                            wm(v, f"c5hg{cw-1}_{IC//8-1}")
                        pbuf = pa if oc % 2 == 0 else pb
                        vs.inc(v.tensor_tensor(mf2r[:, oc % 8, :], pbuf[:, 0:MW],
                                               ivb[:, :], OP.mult))
                        vs.wait(v)
                        mark(f"c5sc{cw}_{oc}", vs)

                @block.scalar
                def _(sc, cw=cw, g=g):
                    for k in range(8):
                        oc = 8 * g + k
                        wm(sc, f"c5sc{cw}_{oc}")
                        ss.inc(sc.activation(hbuf[:, oc, :], mf2r[:, oc % 8, :],
                                             AF.Gelu, bias=b1pp[:, oc:oc + 1]))
                        mark(f"c5h{cw}_{oc}", ss)
                    ss.wait(sc)
                    mark(f"c5hg{cw}_{g}", ss)

                if g > 0:
                    for ocq in range(8 * (g - 1), 8 * g):
                        q2_sections(cw, ocq, m0)

            for ocq in range(IC - 8, IC):
                q2_sections(cw, ocq, m0)

            for tt in range(2):
                @block.tensor
                def _(t, cw=cw, tt=tt):
                    wm(t, f"c5q2f{cw}_{IC-1}")
                    if tt > 0:
                        wm(t, f"c5mx{cw}_{tt-1}")
                    ts.inc(t.transpose(po[:, :], mxq[:, 128 * tt:128 * tt + 128],
                                       idf[:, :]))
                    ts.wait(t)
                    mark(f"c5mt{cw}_{tt}", ts)

                @block.vector
                def _(v, cw=cw, tt=tt):
                    wm(v, f"c5mt{cw}_{tt}")
                    vs.inc(v.tensor_reduce(colbuf[:, 16:17], po[:, :], AX.X, OP.max))
                    vs.wait(v)
                    mark(f"c5mc{cw}_{tt}", vs)

                @block.tensor
                def _(t, cw=cw, tt=tt):
                    wm(t, f"c5mc{cw}_{tt}")
                    ts.inc(t.matmul(pr2[0:1, 128 * tt:128 * tt + 128],
                                    colbuf[:, 16:17], idf[:, :], start=True,
                                    stop=True))
                    ts.wait(t)
                    mark(f"c5mx{cw}_{tt}", ts)

            @block.vector
            def _(v, cw=cw):
                wm(v, f"c5mx{cw}_{1}", f"c5q2m{cw}_{IC-1}")
                vs.inc(v.tensor_scalar(r2(0), pr1[0:1, 0:MW], 1.0 / DI, None,
                                       OP.mult))
                vs.inc(v.tensor_copy(r2(1), pr2[0:1, 0:MW]))
                vs.wait(v)
                mark(f"c5q2r{cw}", vs)

            @block.scalar
            def _(sc, cw=cw):
                wm(sc, f"c5q2r{cw}")
                ss.inc(sc.activation(r2(0), r2(0), AF.Sqrt,
                                     bias=consts[0:1, CEPS6:CEPS6 + 1]))
                ss.inc(sc.activation(r2(1), r2(1), AF.Sqrt))
                ss.wait(sc)
                mark(f"c5q2sd{cw}", ss)

            @block.vector
            def _(v, cw=cw):
                wm(v, f"c5q2sd{cw}")
                vs.inc(v.reciprocal(r2(0), r2(0)))
                vs.wait(v)
                vs.inc(v.tensor_tensor(r2(2), r2(1), r2(0), OP.mult))
                vs.wait(v)
                vs.inc(v.tensor_scalar(r2(2), r2(2), 1e-5, None, OP.max))
                vs.wait(v)
                vs.inc(v.reciprocal(r2(3), r2(2)))
                vs.wait(v)
                vs.inc(v.tensor_scalar(r2(3), r2(3), 127.0, None, OP.mult))
                vs.wait(v)
                vs.inc(v.tensor_tensor(r2(3), r2(3), r2(0), OP.mult))
                vs.wait(v)
                vs.inc(v.tensor_scalar(r2(2), r2(2), 1.0 / 127.0, None, OP.mult))
                vs.wait(v)
                vs.inc(v.tensor_single_scalar(r2(2), r2(2), colbuf[0:1, 21:22],
                                              OP.mult))
                vs.wait(v)
                vs.inc(v.tensor_copy(rwb(0), r2(3)))
                vs.inc(v.tensor_copy(rwb(1), r2(2)))
                vs.wait(v)
                mark(f"c5rows{cw}", vs)

            @block.tensor
            def _(t, cw=cw):
                wm(t, f"c5rows{cw}")
                ts.inc(t.matmul(pc[:, 0:MW], ones_col[:, :], rwb(0), start=True,
                                stop=True))
                ts.wait(t)
                mark(f"c5qbc{cw}", ts)

            @block.vector
            def _(v, cw=cw):
                wm(v, f"c5qbc{cw}")
                vs.inc(v.tensor_copy(qb2[:, :], pc[:, 0:MW]))
                vs.wait(v)
                mark(f"c5qb2{cw}", vs)

            for oc in range(IC):
                @block.vector
                def _(v, cw=cw, oc=oc):
                    wm(v, f"c5qb2{cw}")
                    if cw == 0 and oc == 0:
                        wm(v, "c5negc")
                    mb = mf2 if oc % 2 == 0 else mxq
                    vs.inc(v.tensor_tensor(mb[:, :], hbuf[:, oc, :], qb2[:, :],
                                           OP.mult))
                    vs.wait(v)
                    vs.inc(v.scalar_tensor_tensor(
                        hq[:, oc, :], mb[:, :], RC,
                        negc[:, 0:1].broadcast_to([128, MW]),
                        OP.add, OP.add))
                    vs.wait(v)
                    mark(f"c5hq{cw}_{oc}", vs)

            @block.tensor
            def _(t, cw=cw):
                wm(t, f"c5rows{cw}", f"c5qb2{cw}")
                ts.inc(t.matmul(pc[:, 0:MW], ones_col[:, :], rwb(1), start=True,
                                stop=True))
                ts.wait(t)
                mark(f"c5ivbc{cw}", ts)

            @block.vector
            def _(v, cw=cw):
                wm(v, f"c5ivbc{cw}", f"c5hq{cw}_{IC-1}")
                vs.inc(v.tensor_copy(ivb[:, :], pc[:, 0:MW]))
                vs.wait(v)
                mark(f"c5iv2{cw}", vs)

            for oc2 in range(CC):
                @block.sync
                def _(sy, cw=cw, oc2=oc2):
                    idx = cw * CC + oc2
                    if idx >= 2:
                        pi = idx - 2
                        wm(sy, f"c5w2mm{pi // CC}_{pi % CC}")
                    wc = woa if oc2 % 2 == 0 else wob
                    wc.wait(sy)
                    wc.inc(sy.dma_start(out=w2db[:, oc2 % 2, :],
                                        in_=w2qd.ap()[oc2]), 16)
                    mark(f"c5w2l{cw}_{oc2}", wc)

                @block.tensor
                def _(t, cw=cw, oc2=oc2):
                    wm(t, f"c5w2l{cw}_{oc2}", f"c5hq{cw}_{IC-1}")
                    if oc2 == 0:
                        wm(t, f"c5iv2{cw}")
                    pbuf = pc if oc2 % 2 == 0 else pd
                    idx = cw * CC + oc2
                    if idx >= 2:
                        pi = idx - 2
                        wm(t, f"c5ysc{pi // CC}_{pi % CC}")
                    for i in range(IC):
                        ts.inc(t.matmul(pbuf[:, 0:MW],
                                        w2db[:, oc2 % 2, 128 * i:128 * i + 128],
                                        hq[:, i, :],
                                        start=(i == 0), stop=(i == IC - 1)))
                    mark(f"c5w2mm{cw}_{oc2}", ts)

                @block.vector
                def _(v, cw=cw, oc2=oc2):
                    wm(v, f"c5w2mm{cw}_{oc2}", f"c5iv2{cw}")
                    if cw > 0:
                        wm(v, f"c5g{cw-1}_{oc2}")
                    pbuf = pc if oc2 % 2 == 0 else pd
                    vs.inc(v.tensor_tensor(x1w[:, oc2, :], pbuf[:, 0:MW], ivb[:, :],
                                           OP.mult))
                    vs.wait(v)
                    mark(f"c5ysc{cw}_{oc2}", vs)

            @block.scalar
            def _(sc, cw=cw):
                for oc2 in range(CC):
                    wm(sc, f"c5ysc{cw}_{oc2}")
                    ss.inc(sc.activation(gcw[:, oc2, :], x1w[:, oc2, :], AF.Silu,
                                         bias=cv("b2l1", oc2)))
                    mark(f"c5g{cw}_{oc2}", ss)
                ss.wait(sc)

            @block.sync
            def _(sy, cw=cw, m0=m0):
                ld.wait(sy)
                wm(sy, f"c5g{cw}_{CC-1}")
                for c_ in range(CC):
                    ld.inc(sy.dma_start(out=x1w[:, c_, :],
                                        in_=x1pbd.ap()[128 * c_:128 * c_ + 128,
                                                       m0:m0 + MW]), 16)
                mark(f"c5x1{cw}", ld)

            @block.vector
            def _(v, cw=cw, m0=m0):
                wm(v, f"c5x1{cw}")
                for c_ in range(CC):
                    vs.inc(v.tensor_tensor(gcw[:, c_, :], sg_bf[:, c_, m0:m0 + MW],
                                           gcw[:, c_, :], OP.mult))
                    vs.wait(v)
                    vs.inc(v.tensor_tensor(x1w[:, c_, :], x1w[:, c_, :],
                                           gcw[:, c_, :], OP.add))
                    vs.wait(v)
                mark(f"c5out{cw}", vs)

            for sub in range(2):
                for fc in range(CC):
                    @block.tensor
                    def _(t, cw=cw, sub=sub, fc=fc):
                        wm(t, f"c5out{cw}")
                        if fc > 0 or sub > 0 or cw > 0:
                            wm(t, "c5ot_prev")
                        ts.inc(t.transpose(po[:, :],
                                           x1w[:, fc, 128 * sub:128 * sub + 128],
                                           idf[:, :]))
                        mark("c5ot_cur", ts)

                    @block.scalar
                    def _(sc, cw=cw, sub=sub, fc=fc):
                        wm(sc, "c5ot_cur")
                        ss.inc(sc.copy(outT[:, 128 * fc:128 * fc + 128], po[:, :]))
                        ss.wait(sc)
                        mark("c5ot_prev", ss)

                @block.scalar
                def _(sc, cw=cw, sub=sub, m0=m0):
                    wm(sc, "c5ot_prev")
                    dst = AP(tensor=out_c.ap().tensor,
                             offset=(m0 + 128 * sub) // 8 * D * 8 // 8 * 1
                             if False else (32 * cw + 16 * sub) * D,
                             ap=[[D, 16], [W * D, 8], [1, D]])
                    od.inc(sc.dma_start(out=dst, in_=outT[:, :]), 16)
                    od.wait(sc)
                    mark(f"c5st{cw}_{sub}", od)

        barrier()
        est.close()
        return nc

    if not PHASE_C:
        xsl = A("xsl", [128, S], F32)
        xpb = A("xpb", [128, S], F32)
        oev = A("oev", [128, 128], F32)

        for c_ in range(CC):
            @block.gpsimd
            def _(g, c_=c_):
                wm(g, "ag_done")
                if c_ > 0:
                    wm(g, f"sxpb{c_-1}")
                pid = g.partition_id()
                off = pid * W + c_ * 128 * S
                src = AP(tensor=x1all.ap().tensor, offset=off,
                         ap=[[S, 128], [D * S, B], [1, W]])
                xd.inc(g.dma_start(out=xsl[:, :], in_=src), 16)
                xd.wait(g)
                gs.inc(g.nop())
                mark(f"sxsl{c_}", gs)

            @block.vector
            def _(v, c_=c_):
                wm(v, f"sxsl{c_}")
                if c_ > 0:
                    wm(v, f"sotr{c_-1}_15")
                vs.inc(v.tensor_copy(
                    xpb[:, :].rearrange("q (p b) -> q b p", b=B),
                    xsl[:, :].rearrange("q (b p) -> q b p", b=B)))
                vs.wait(v)
                mark(f"sxpb{c_}", vs)

            for tc in range(16):
                @block.tensor
                def _(t, c_=c_, tc=tc):
                    wm(t, f"sxpb{c_}")
                    if (c_, tc) != (0, 0):
                        pcc, ptc = (c_, tc - 1) if tc > 0 else (c_ - 1, 15)
                        wm(t, f"soev{pcc}_{ptc}")
                    ts.inc(t.transpose(po[:, :], xpb[:, 128 * tc:128 * tc + 128],
                                       idf[:, :]))
                    ts.wait(t)
                    mark(f"sotr{c_}_{tc}", ts)

                @block.scalar
                def _(sc, c_=c_, tc=tc):
                    wm(sc, f"sotr{c_}_{tc}")
                    ss.inc(sc.copy(oev[:, :], po[:, :]))
                    ss.wait(sc)
                    mark(f"soev{c_}_{tc}", ss)
                    dst = AP(tensor=out_c.ap().tensor, offset=16 * tc * D + 128 * c_,
                             ap=[[D, 16], [W * D, 8], [1, 128]])
                    od.inc(sc.dma_start(out=dst, in_=oev[:, :]), 16)
                    od.wait(sc)

        barrier()
        est.close()
        return nc

    build_phase_c(locals())
    est.close()
    return nc


def build_phase_c(env):
    raise NotImplementedError("phase C emitted in next revision")


def kernel(**inputs):
    in_maps = host_prep(inputs)
    nc = build_nc()
    res = run_bass_kernel_spmd(nc, in_maps, core_ids=list(range(NCORES)))
    out = np.concatenate([res.results[c]["out_c"] for c in range(NCORES)], axis=1)
    return np.ascontiguousarray(out.astype(np.float32))
